# version 07
# baseline (speedup 1.0000x reference)
import numpy as np
from contextlib import ExitStack
import concourse.bass as bass
import concourse.mybir as mybir
from concourse.bass_utils import run_bass_kernel_spmd

F32 = mybir.dt.float32
BF16 = mybir.dt.bfloat16
U8 = mybir.dt.uint8
ALU = mybir.AluOpType
AF = mybir.ActivationFunctionType
AXF = mybir.AxisListType.X

B, L, V, DIM, DFF, HD, DEPTH = 2, 4096, 4096, 512, 2048, 11008, 4
T, TH = 1024, 512
NKV, NKD, NKF, NKH = V // 128, DIM // 128, DFF // 128, HD // 128
EPS = 1e-5
PAR_NAMES = ["ln1w", "ln1b", "ln2w", "ln2b", "pre", "pim", "lre", "lim", "scl"]
DBLK = (HD + 511) // 512


STALLS = set()


def build_nc(dbg=False):
    nc = bass.Bass("TRN2", target_bir_lowering=False)
    dr = {}

    def din(name, shape, dt):
        dr[name] = nc.dram_tensor(name, shape, dt, kind="ExternalInput").ap()

    din("xT", [V, T], F32)
    din("selv", [128, 4], F32)
    din("winT", [128, NKV * DIM], F32)
    din("w1T", [DEPTH, 128, NKD * DFF], F32)
    din("w2T", [DEPTH, 128, NKF * DIM], F32)
    din("woutT", [128, NKD * V], F32)
    din("par", [128, 9 * NKD * DEPTH], F32)
    din("lnl", [128, 2 * NKD], F32)
    din("bin", [128, NKD], F32)
    din("bout", [128, NKV], F32)
    din("b1", [128, NKF * DEPTH], F32)
    din("b2", [128, NKD * DEPTH], F32)
    din("qg", [128, NKH, NKV * 128], U8)
    din("qu", [128, NKH, NKV * 128], U8)
    din("qd", [128, NKV, NKH * 128], U8)
    din("sgw", [NKH, V], F32)
    din("suw", [NKH, V], F32)
    din("sdw", [NKV, HD], F32)
    din("zg", [32, HD], U8)
    din("zu", [32, HD], U8)
    din("zd", [NKH, V], U8)
    din("sg", [32, HD], F32)
    din("su", [32, HD], F32)
    din("sd", [NKH, V], F32)
    din("e32", [128, 32 * 32], F32)
    din("e86", [128, NKH * NKH], F32)
    din("cst", [128, 2], F32)
    din("onc", [128, 1], F32)
    din("onr", [1, 128], F32)

    outT = nc.dram_tensor("outT", [V, T], F32, kind="ExternalOutput").ap()
    dbg_out = {}
    if dbg:
        for nm, shp in [("dbg_h", [128, NKD * T]), ("dbg_x1", [128, NKV * T]),
                        ("dbg_y", [128, NKD * T]), ("dbg_sa", [128, NKD * T]),
                        ("dbg_h0", [128, NKD * T]), ("dbg_hd0", [128, NKD * T]),
                        ("dbg_ya", [128, NKD * T]), ("dbg_sc", [128, NKD * T])]:
            dbg_out[nm] = nc.dram_tensor(nm, shp, F32, kind="ExternalOutput").ap()

    hsp = [nc.dram_tensor(f"hsp{i}", [128, NKH, TH], BF16, kind="Internal") for i in range(2)]
    x1d = nc.dram_tensor("x1d", [128, NKV, T], BF16, kind="Internal")
    xrd = nc.dram_tensor("xrd", [128, NKV, T], BF16, kind="Internal")
    xfd = [nc.dram_tensor(f"xfd{i}", [128, NKV, TH], BF16, kind="Internal") for i in range(2)]
    nzg_d = nc.dram_tensor("nzg", [32, HD], BF16, kind="Internal")
    nzu_d = nc.dram_tensor("nzu", [32, HD], BF16, kind="Internal")
    nzd_d = nc.dram_tensor("nzd", [NKH, V], BF16, kind="Internal")
    cc_in = nc.dram_tensor("cc_in", [128, 8], F32, kind="Internal")
    cc_out = nc.dram_tensor("cc_out", [512, 8], F32, kind="Internal")

    es = ExitStack()

    def mk(stack, name, shape, dt):
        return stack.enter_context(nc.sbuf_tensor(name, shape, dt))

    s_par = mk(es, "s_par", [128, 9 * NKD * DEPTH], F32)
    s_lnl = mk(es, "s_lnl", [128, 2 * NKD], F32)
    s_bin = mk(es, "s_bin", [128, NKD], F32)
    s_bout = mk(es, "s_bout", [128, NKV], F32)
    s_b1 = mk(es, "s_b1", [128, NKF * DEPTH], F32)
    s_b2 = mk(es, "s_b2", [128, NKD * DEPTH], F32)
    s_cst = mk(es, "s_cst", [128, 2], F32)
    s_onc = mk(es, "s_onc", [128, 1], BF16)
    s_oncf = mk(es, "s_oncf", [128, 1], F32)
    s_onrb = mk(es, "s_onrb", [1, 128], BF16)
    s_onrf = mk(es, "s_onrf", [1, 128], F32)
    s_selv = mk(es, "s_selv", [128, 4], F32)
    s_e32 = mk(es, "s_e32", [128, 32 * 32], BF16)
    s_row = mk(es, "s_row", [1, 5 * TH], F32)
    s_tiny = mk(es, "s_tiny", [128, 256], F32)
    s_h = mk(es, "s_h", [128, NKD * T], F32)
    e_w = ExitStack()
    s_wsh = mk(e_w, "s_wsh", [128, 16384], BF16)

    P = [es.enter_context(nc.psum_tensor(f"pb{i}", [128, 512], F32)) for i in range(8)]

    sems = {}
    for s in ["ld", "lg", "pe", "ve", "sc", "cc", "st"]:
        sems[s] = es.enter_context(nc.semaphore(s))
    cnt = {s: 0 for s in sems}
    block = es.enter_context(nc.Block())

    def inc(ins, s, k=1):
        ins.then_inc(sems[s], k)
        cnt[s] += k
        return cnt[s]

    def barrier(name):
        if name not in STALLS:
            return
        vals = dict(cnt)
        for deco in [block.tensor, block.vector, block.scalar, block.sync, block.gpsimd]:
            @deco
            def _(e, vals=vals):
                for s_, v_ in vals.items():
                    e.wait_ge(sems[s_], v_)

    def dmai(eng, out, in_, s="ld", slow=False):
        eng.wait_ge(sems[s], cnt[s])
        return inc(eng.dma_start(out=out, in_=in_,
                                 allow_slow_non_contiguous=slow), s, 16)

    def par(nm, dt_, lyr):
        i = PAR_NAMES.index(nm)
        c = (i * NKD + dt_) * DEPTH + lyr
        return s_par[:, c:c + 1]

    @block.sync
    def _(sy):
        for t_, src in [(s_par, "par"), (s_lnl, "lnl"), (s_bin, "bin"),
                        (s_bout, "bout"), (s_b1, "b1"), (s_b2, "b2"),
                        (s_onrf, "onr"), (s_selv, "selv"), (s_cst, "cst"),
                        (s_oncf, "onc")]:
            dmai(sy, t_[:], dr[src])
    ld_small = cnt["ld"]

    @block.gpsimd
    def _(g):
        dmai(g, s_onc[:], dr["onc"], "lg")
        dmai(g, s_onrb[:], dr["onr"], "lg")
        dmai(g, s_e32[:], dr["e32"], "lg")
    lg_small = cnt["lg"]

    with ExitStack() as e0:
        s_zf = mk(e0, "s_zf", [NKH, HD], F32)
        s_sf = mk(e0, "s_sf", [NKH, HD], F32)
        s_nz = mk(e0, "s_nz", [NKH, HD], BF16)
        for wi, (zn, sn, dst, pp) in enumerate([("zg", "sg", nzg_d, 32),
                                                ("zu", "su", nzu_d, 32),
                                                ("zd", "sd", nzd_d, NKH)]):
            nn = (32 * HD) // pp
            st_pre = cnt["st"]

            @block.gpsimd
            def _(g, zn=zn, wi=wi, pp=pp, nn=nn, st_pre=st_pre):
                if wi > 0:
                    g.wait_ge(sems["st"], st_pre)
                dmai(g, s_zf[0:pp, 0:nn], dr[zn], "lg")
            lg_z = cnt["lg"]

            @block.sync
            def _(sy, sn=sn, wi=wi, pp=pp, nn=nn, st_pre=st_pre):
                if wi > 0:
                    sy.wait_ge(sems["st"], st_pre)
                dmai(sy, s_sf[0:pp, 0:nn], dr[sn])
            ld_z = cnt["ld"]

            @block.vector
            def _(v, pp=pp, nn=nn, lg_z=lg_z, ld_z=ld_z):
                v.wait_ge(sems["lg"], lg_z)
                v.wait_ge(sems["ld"], ld_z)
                inc(v.scalar_tensor_tensor(out=s_nz[0:pp, 0:nn], in0=s_zf[0:pp, 0:nn],
                                           scalar=-1.0, in1=s_sf[0:pp, 0:nn],
                                           op0=ALU.mult, op1=ALU.mult), "ve")
            ve_z = cnt["ve"]

            @block.sync
            def _(sy, dst=dst, pp=pp, nn=nn, ve_z=ve_z):
                sy.wait_ge(sems["ve"], ve_z)
                dmai(sy, dst.ap(), s_nz[0:pp, 0:nn], "st")
    st_nz = cnt["st"]

    def ln_stats_rows(src_b, src_2b, nk, nd, waits):
        @block.tensor
        def _(t):
            t.wait_ge(sems["ld"], ld_small)
            t.wait_ge(sems["lg"], lg_small)
            for s_, v_ in waits:
                t.wait_ge(sems[s_], v_)
            for n in range(2):
                pr = P[2] if n == 0 else P[5]
                for which, src in [(0, src_b), (1, src_2b)]:
                    r0 = which * 32
                    for k in range(nk):
                        ins = t.matmul(pr[r0:r0 + 1, :], s_oncf[:],
                                       src[:, k * T + n * TH:k * T + n * TH + TH],
                                       start=(k == 0), stop=(k == nk - 1))
                    inc(ins, "pe")
        pe_r = cnt["pe"]

        @block.vector
        def _(v):
            v.wait_ge(sems["pe"], pe_r)
            for n in range(2):
                pr = P[2] if n == 0 else P[5]
                v.tensor_scalar(out=s_row[0:1, n * TH:n * TH + TH],
                                in0=pr[0:1, :], scalar1=1.0 / nd,
                                scalar2=None, op0=ALU.mult)
                v.tensor_scalar(out=s_row[0:1, 1024 + n * TH:1024 + n * TH + TH],
                                in0=pr[32:33, :], scalar1=1.0 / nd,
                                scalar2=None, op0=ALU.mult)
            v.drain()
            for n in range(2):
                v.tensor_tensor(out=s_row[0:1, 2048:2048 + TH],
                                in0=s_row[0:1, n * TH:n * TH + TH],
                                in1=s_row[0:1, n * TH:n * TH + TH], op=ALU.mult)
                v.drain()
                ins = v.tensor_tensor(out=s_row[0:1, 1024 + n * TH:1024 + n * TH + TH],
                                      in0=s_row[0:1, 1024 + n * TH:1024 + n * TH + TH],
                                      in1=s_row[0:1, 2048:2048 + TH],
                                      op=ALU.subtract)
                v.drain()
            inc(ins, "ve")
        ve_v = cnt["ve"]

        @block.scalar
        def _(s):
            s.wait_ge(sems["ve"], ve_v)
            for n in range(2):
                ins = s.activation(out=s_row[0:1, 1024 + n * TH:1024 + n * TH + TH],
                                   in_=s_row[0:1, 1024 + n * TH:1024 + n * TH + TH],
                                   func=AF.Sqrt, bias=s_cst[0:1, 0:1], scale=1.0)
            inc(ins, "sc")
        sc_q = cnt["sc"]

        @block.vector
        def _(v):
            v.wait_ge(sems["sc"], sc_q)
            for n in range(2):
                ins = v.reciprocal(out=s_row[0:1, 1024 + n * TH:1024 + n * TH + TH],
                                   in_=s_row[0:1, 1024 + n * TH:1024 + n * TH + TH])
            inc(ins, "ve")
        return cnt["ve"]

    def ln_bcast(n, ve_rows, extra_waits=()):
        @block.tensor
        def _(t):
            t.wait_ge(sems["ld"], ld_small)
            t.wait_ge(sems["ve"], ve_rows)
            for s_, v_ in extra_waits:
                t.wait_ge(sems[s_], v_)
            t.matmul(P[3][:], s_onrf[:], s_row[0:1, n * TH:n * TH + TH],
                     start=True, stop=True)
            inc(t.matmul(P[4][:], s_onrf[:], s_row[0:1, 1024 + n * TH:1024 + n * TH + TH],
                         start=True, stop=True), "pe")
        return cnt["pe"]

    with ExitStack() as e1:
        s_winf = mk(e1, "s_winf", [128, NKV * DIM], F32)
        s_xs = [mk(e1, f"s_xs_{i}", [128, T], F32) for i in range(2)]

        @block.sync
        def _(sy):
            sy.wait_ge(sems["st"], st_nz)
            for q in range(4):
                dmai(sy, s_winf[:, q * 4096:(q + 1) * 4096],
                     dr["winT"][:, q * 4096:(q + 1) * 4096])
        ld_win = cnt["ld"]
        pe_s00 = cnt["pe"]
        ld_xs0 = cnt["ld"]

        @block.sync
        def _(sy):
            for k in range(NKV):
                if k >= 2:
                    sy.wait_ge(sems["pe"], pe_s00 + k - 1)
                dmai(sy, s_xs[k % 2][:], dr["xT"][k * 128:(k + 1) * 128, :])

        @block.tensor
        def _(t):
            t.wait_ge(sems["ld"], ld_win)
            for k in range(NKV):
                t.wait_ge(sems["ld"], ld_xs0 + (k + 1) * 16)
                for m in range(NKD):
                    for n in range(2):
                        ins = t.matmul(P[m * 2 + n][:],
                                       s_winf[:, k * DIM + m * 128:k * DIM + m * 128 + 128],
                                       s_xs[k % 2][:, n * TH:n * TH + TH],
                                       start=(k == 0), stop=(k == NKV - 1))
                inc(ins, "pe")
        pe_s0 = cnt["pe"]

        @block.scalar
        def _(s):
            s.wait_ge(sems["ld"], ld_small)
            s.wait_ge(sems["pe"], pe_s0)
            for m in range(NKD):
                for n in range(2):
                    inc(s.activation(out=s_h[:, m * T + n * TH:m * T + n * TH + TH],
                                     in_=P[m * 2 + n][:], func=AF.Identity,
                                     bias=s_bin[:, m:m + 1], scale=1.0), "sc")
    sc_h0 = cnt["sc"]
    barrier("s0")

    e2 = ExitStack()
    s_sa = mk(e2, "s_sa", [128, NKD * T], F32)
    s_s2 = mk(e2, "s_s2", [128, NKD * T], F32)
    s_y = mk(e2, "s_y", [128, NKD * T], F32)
    s_scr = mk(e2, "s_scr", [128, 8192], F32)
    s_wab = mk(e2, "s_wab", [128, 2048], F32)
    s_wa = s_wab[:, 0:1024]
    s_wb = s_wab[:, 1024:2048]
    s_wf = mk(e2, "s_wf", [128, 8192], F32)
    s_h2f = s_scr[:, 0:4096]
    s_ybf = mk(e2, "s_ybf", [128, NKD * T], BF16)
    s_eg = mk(e2, "s_eg", [128, 32], F32)
    s_ep = mk(e2, "s_ep", [128, 8], F32)
    s_cand = mk(e2, "s_cand", [128, 64], F32)

    class _DrainV:
        def __init__(self, v):
            self._v = v

        def __getattr__(self, name):
            f = getattr(self._v, name)
            if name in ("drain", "wait_ge"):
                return f

            def wrap(*a, **k):
                r = f(*a, **k)
                self._v.drain()
                return r
            return wrap

    def tcol(dt_, c):
        return s_tiny[:, dt_ * 56 + c:dt_ * 56 + c + 1]

    def cgt(dt_):
        return s_scr[:, dt_ * 2048:dt_ * 2048 + 1024]

    def sgt(dt_):
        return s_scr[:, dt_ * 2048 + 1024:dt_ * 2048 + 2048]

    def rt(dt_):
        return s_y[:, dt_ * 1024:(dt_ + 1) * 1024]

    def scal_chain(lyr):
        @block.vector
        def _(v):
            v.wait_ge(sems["ld"], ld_small)
            for dt_ in range(NKD):
                v.tensor_scalar(out=tcol(dt_, 51), in0=par("pre", dt_, lyr),
                                scalar1=par("pre", dt_, lyr), scalar2=None, op0=ALU.mult)
            v.drain()
            for dt_ in range(NKD):
                ins = v.scalar_tensor_tensor(out=tcol(dt_, 51), in0=par("pim", dt_, lyr),
                                             scalar=par("pim", dt_, lyr), in1=tcol(dt_, 51),
                                             op0=ALU.mult, op1=ALU.add)
            inc(ins, "ve")
        c1 = cnt["ve"]

        @block.scalar
        def _(s):
            s.wait_ge(sems["ve"], c1)
            for dt_ in range(NKD):
                s.activation(out=tcol(dt_, 51), in_=tcol(dt_, 51), func=AF.Sqrt,
                             bias=s_cst[:, 1:2], scale=1.0)
            s.drain()
            for dt_ in range(NKD):
                ins = s.activation(out=tcol(dt_, 43), in_=tcol(dt_, 51), func=AF.Tanh)
            inc(ins, "sc")
        c2 = cnt["sc"]

        @block.vector
        def _(v):
            v.wait_ge(sems["sc"], c2)
            for dt_ in range(NKD):
                v.reciprocal(out=tcol(dt_, 52), in_=tcol(dt_, 51))
            v.drain()
            for dt_ in range(NKD):
                v.tensor_scalar(out=tcol(dt_, 47), in0=par("pre", dt_, lyr),
                                scalar1=tcol(dt_, 52), scalar2=None, op0=ALU.mult)
                v.tensor_scalar(out=tcol(dt_, 48), in0=par("pim", dt_, lyr),
                                scalar1=tcol(dt_, 52), scalar2=None, op0=ALU.mult)
            v.drain()
            for dt_ in range(NKD):
                v.tensor_copy(out=tcol(dt_, 0), in_=tcol(dt_, 47))
                v.tensor_copy(out=tcol(dt_, 11), in_=tcol(dt_, 48))
                v.tensor_scalar(out=tcol(dt_, 22), in0=tcol(dt_, 48), scalar1=-1.0,
                                scalar2=None, op0=ALU.mult)
                v.tensor_copy(out=tcol(dt_, 32), in_=tcol(dt_, 43))
            v.drain()
            for k in range(1, 11):
                for dt_ in range(NKD):
                    v.tensor_scalar(out=tcol(dt_, 53), in0=tcol(dt_, k - 1),
                                    scalar1=tcol(dt_, k - 1), scalar2=None, op0=ALU.mult)
                v.drain()
                for dt_ in range(NKD):
                    v.scalar_tensor_tensor(out=tcol(dt_, k), in0=tcol(dt_, 11 + k - 1),
                                           scalar=tcol(dt_, 22 + k - 1), in1=tcol(dt_, 53),
                                           op0=ALU.mult, op1=ALU.add)
                    v.tensor_scalar(out=tcol(dt_, 11 + k), in0=tcol(dt_, 11 + k - 1),
                                    scalar1=tcol(dt_, k - 1), scalar2=2.0,
                                    op0=ALU.mult, op1=ALU.mult)
                    v.tensor_scalar(out=tcol(dt_, 32 + k), in0=tcol(dt_, 32 + k - 1),
                                    scalar1=tcol(dt_, 32 + k - 1), scalar2=None, op0=ALU.mult)
                v.drain()
                if k <= 9:
                    for dt_ in range(NKD):
                        v.tensor_scalar(out=tcol(dt_, 22 + k), in0=tcol(dt_, 11 + k),
                                        scalar1=-1.0, scalar2=None, op0=ALU.mult)
                    v.drain()
            for dt_ in range(NKD):
                q = dt_ * 16
                v.memset(s_cand[:, q:q + 1], 1.0)
                v.memset(s_cand[:, q + 4:q + 5], 0.0)
                v.tensor_copy(out=s_cand[:, q + 1:q + 2], in_=tcol(dt_, 10))
                v.tensor_copy(out=s_cand[:, q + 5:q + 6], in_=tcol(dt_, 21))
                v.memset(s_cand[:, q + 8:q + 9], 1.0)
                v.tensor_copy(out=s_cand[:, q + 9:q + 10], in_=tcol(dt_, 42))
                v.tensor_scalar(out=s_cand[:, q + 10:q + 11], in0=tcol(dt_, 42),
                                scalar1=tcol(dt_, 42), scalar2=None, op0=ALU.mult)
            v.drain()
            for j in (2, 3):
                for dt_ in range(NKD):
                    q = dt_ * 16
                    v.tensor_scalar(out=tcol(dt_, 53), in0=s_cand[:, q + j - 1:q + j],
                                    scalar1=tcol(dt_, 10), scalar2=None, op0=ALU.mult)
                    v.tensor_scalar(out=tcol(dt_, 54), in0=s_cand[:, q + 3 + j:q + 4 + j],
                                    scalar1=tcol(dt_, 21), scalar2=None, op0=ALU.mult)
                v.drain()
                for dt_ in range(NKD):
                    q = dt_ * 16
                    v.tensor_tensor(out=s_cand[:, q + j:q + j + 1], in0=tcol(dt_, 53),
                                    in1=tcol(dt_, 54), op=ALU.subtract)
                    v.tensor_scalar(out=tcol(dt_, 53), in0=s_cand[:, q + 3 + j:q + 4 + j],
                                    scalar1=tcol(dt_, 10), scalar2=None, op0=ALU.mult)
                    v.tensor_scalar(out=tcol(dt_, 54), in0=s_cand[:, q + j - 1:q + j],
                                    scalar1=tcol(dt_, 21), scalar2=None, op0=ALU.mult)
                v.drain()
                for dt_ in range(NKD):
                    q = dt_ * 16
                    v.tensor_tensor(out=s_cand[:, q + 4 + j:q + 5 + j], in0=tcol(dt_, 53),
                                    in1=tcol(dt_, 54), op=ALU.add)
                    if j == 3:
                        v.tensor_scalar(out=s_cand[:, q + 11:q + 12],
                                        in0=s_cand[:, q + 10:q + 11],
                                        scalar1=tcol(dt_, 42), scalar2=None, op0=ALU.mult)
                v.drain()
            for dt_ in range(NKD):
                q = dt_ * 16
                v.tensor_tensor(out=s_cand[:, q:q + 4], in0=s_cand[:, q:q + 4],
                                in1=s_selv[:], op=ALU.mult)
                v.tensor_tensor(out=s_cand[:, q + 4:q + 8], in0=s_cand[:, q + 4:q + 8],
                                in1=s_selv[:], op=ALU.mult)
                v.tensor_tensor(out=s_cand[:, q + 8:q + 12], in0=s_cand[:, q + 8:q + 12],
                                in1=s_selv[:], op=ALU.mult)
            v.drain()
            for dt_ in range(NKD):
                q = dt_ * 16
                v.tensor_reduce(out=tcol(dt_, 44), in_=s_cand[:, q:q + 4], axis=AXF, op=ALU.add)
                v.tensor_reduce(out=tcol(dt_, 45), in_=s_cand[:, q + 4:q + 8], axis=AXF, op=ALU.add)
                v.tensor_reduce(out=tcol(dt_, 46), in_=s_cand[:, q + 8:q + 12], axis=AXF, op=ALU.add)
            v.drain()
            for dt_ in range(NKD):
                v.tensor_scalar(out=tcol(dt_, 53), in0=par("lre", dt_, lyr),
                                scalar1=tcol(dt_, 47), scalar2=None, op0=ALU.mult)
                v.tensor_scalar(out=tcol(dt_, 54), in0=par("lim", dt_, lyr),
                                scalar1=tcol(dt_, 48), scalar2=None, op0=ALU.mult)
                v.tensor_scalar(out=tcol(dt_, 55), in0=par("lre", dt_, lyr),
                                scalar1=tcol(dt_, 48), scalar2=None, op0=ALU.mult)
                v.tensor_scalar(out=tcol(dt_, 51), in0=par("lim", dt_, lyr),
                                scalar1=tcol(dt_, 47), scalar2=None, op0=ALU.mult)
            v.drain()
            for dt_ in range(NKD):
                v.tensor_tensor(out=tcol(dt_, 49), in0=tcol(dt_, 53), in1=tcol(dt_, 54),
                                op=ALU.subtract)
                ins = v.tensor_tensor(out=tcol(dt_, 50), in0=tcol(dt_, 55), in1=tcol(dt_, 51),
                                      op=ALU.add)
            v.drain()
            inc(ins, "ve")

    def tab_pass(want_r, pre_wait=None):
        @block.vector
        def _(v):
            if pre_wait is not None:
                v.wait_ge(sems[pre_wait[0]], pre_wait[1])
            for dt_ in range(NKD):
                v.tensor_copy(out=cgt(dt_)[:, 0:1], in_=tcol(dt_, 44))
                v.tensor_copy(out=sgt(dt_)[:, 0:1], in_=tcol(dt_, 45))
                if want_r:
                    v.memset(rt(dt_)[:, 0:1], 1.0)
            v.drain()
            for k in range(10):
                w = 1 << k
                for dt_ in range(NKD):
                    v.tensor_scalar(out=s_wab[:, dt_ * 512:dt_ * 512 + w], in0=cgt(dt_)[:, 0:w],
                                    scalar1=tcol(dt_, k), scalar2=None, op0=ALU.mult)
                v.drain()
                for dt_ in range(NKD):
                    v.scalar_tensor_tensor(out=cgt(dt_)[:, w:2 * w], in0=sgt(dt_)[:, 0:w],
                                           scalar=tcol(dt_, 22 + k),
                                           in1=s_wab[:, dt_ * 512:dt_ * 512 + w],
                                           op0=ALU.mult, op1=ALU.add)
                v.drain()
                for dt_ in range(NKD):
                    v.tensor_scalar(out=s_wab[:, dt_ * 512:dt_ * 512 + w], in0=cgt(dt_)[:, 0:w],
                                    scalar1=tcol(dt_, 11 + k), scalar2=None, op0=ALU.mult)
                v.drain()
                for dt_ in range(NKD):
                    v.scalar_tensor_tensor(out=sgt(dt_)[:, w:2 * w], in0=sgt(dt_)[:, 0:w],
                                           scalar=tcol(dt_, k),
                                           in1=s_wab[:, dt_ * 512:dt_ * 512 + w],
                                           op0=ALU.mult, op1=ALU.add)
                    if want_r:
                        v.tensor_scalar(out=rt(dt_)[:, w:2 * w], in0=rt(dt_)[:, 0:w],
                                        scalar1=tcol(dt_, 32 + k), scalar2=None, op0=ALU.mult)
                ins = v.drain()
            inc(v.engine_nop(), "ve")

    def ln_norm_affine(dst, func, wname, bname, lyr, ve_rows, w_is_lnl=False,
                       pre_waits=()):
        for n in range(2):
            extra = list(pre_waits) if n == 0 else [("sc", cnt["sc"])]
            pe_bc = ln_bcast(n, ve_rows, extra_waits=extra)

            @block.vector
            def _(v, n=n, pe_bc=pe_bc):
                v.wait_ge(sems["pe"], pe_bc)
                for dt_ in range(NKD):
                    sl = slice(dt_ * T + n * TH, dt_ * T + n * TH + TH)
                    v.tensor_tensor(out=s_y[:, sl], in0=s_h[:, sl], in1=P[3][:],
                                    op=ALU.subtract)
                v.drain()
                for dt_ in range(NKD):
                    sl = slice(dt_ * T + n * TH, dt_ * T + n * TH + TH)
                    ins = v.tensor_tensor(out=s_y[:, sl], in0=s_y[:, sl], in1=P[4][:],
                                          op=ALU.mult)
                inc(ins, "ve")
            ve_n = cnt["ve"]

            @block.scalar
            def _(s, n=n, ve_n=ve_n):
                s.wait_ge(sems["ve"], ve_n)
                for dt_ in range(NKD):
                    sl = slice(dt_ * T + n * TH, dt_ * T + n * TH + TH)
                    if w_is_lnl:
                        sc_, bi_ = s_lnl[:, dt_:dt_ + 1], s_lnl[:, NKD + dt_:NKD + dt_ + 1]
                    else:
                        sc_, bi_ = par(wname, dt_, lyr), par(bname, dt_, lyr)
                    ins = s.activation(out=dst[:, sl], in_=s_y[:, sl], func=func,
                                       scale=sc_, bias=bi_)
                inc(ins, "sc")
        return cnt["sc"]

    if dbg:
        @block.sync
        def _(sy):
            sy.wait_ge(sems["sc"], sc_h0)
            dmai(sy, dbg_out["dbg_h0"], s_h[:], "st")


        @block.vector
        def _(v):
            v.wait_ge(sems["st"], cnt["st"])
    @block.gpsimd
    def _(g):
        dmai(g, s_wsh[:], dr["woutT"], "lg")
    lg_wout = cnt["lg"]

    for lyr in range(DEPTH):
        @block.scalar
        def _(s, lyr=lyr):
            s.wait_ge(sems["ve"], cnt["ve"])
            if lyr == 0:
                s.wait_ge(sems["sc"], sc_h0)
            inc(s.activation(out=s_h2f[:], in_=s_h[:], func=AF.Square), "sc")
        ve_rows = ln_stats_rows(s_h, s_h2f, NKD, DIM, [("sc", cnt["sc"])])
        sc_y = ln_norm_affine(s_y, AF.Silu, "ln1w", "ln1b", lyr, ve_rows)

        if dbg and lyr == 0:
            @block.sync
            def _(sy):
                sy.wait_ge(sems["sc"], sc_y)
                dmai(sy, dbg_out["dbg_ya"], s_y[:], "st")


            @block.vector
            def _(v):
                v.wait_ge(sems["st"], cnt["st"])
        scal_chain(lyr)
        tab_pass(False, pre_wait=("sc", sc_y))
        for dt_ in range(NKD):
            @block.vector
            def _(v, dt_=dt_, lyr=lyr):
                v.tensor_tensor(out=s_wa[:], in0=cgt(dt_), in1=s_y[:, dt_ * T:(dt_ + 1) * T],
                                op=ALU.mult)
                v.tensor_tensor(out=s_wb[:], in0=sgt(dt_), in1=s_y[:, dt_ * T:(dt_ + 1) * T],
                                op=ALU.mult)
                v.drain()
                dec = tcol(dt_, 43).broadcast_to((128, TH))
                for (src, dst2) in [(s_wa, s_sa), (s_wb, s_s2)]:
                    v.tensor_tensor_scan(out=dst2[:, dt_ * T:dt_ * T + TH], data0=dec,
                                         data1=src[:, 0:TH], initial=0.0,
                                         op0=ALU.mult, op1=ALU.add)
                    v.drain()
                    ins = v.tensor_tensor_scan(out=dst2[:, dt_ * T + TH:dt_ * T + T], data0=dec,
                                               data1=src[:, TH:T],
                                               initial=dst2[:, dt_ * T + TH - 1:dt_ * T + TH],
                                               op0=ALU.mult, op1=ALU.add)
                    v.drain()
                v.tensor_copy(out=s_ep[:, dt_:dt_ + 1], in_=s_sa[:, dt_ * T + T - 1:dt_ * T + T])
                ins = v.tensor_copy(out=s_ep[:, 4 + dt_:5 + dt_],
                                    in_=s_s2[:, dt_ * T + T - 1:dt_ * T + T])
                inc(ins, "ve")
            ve_scan = cnt["ve"]
        @block.sync
        def _(sy, lyr=lyr, ve_scan=ve_scan):
            sy.wait_ge(sems["ve"], ve_scan)
            if lyr > 0:
                sy.wait_ge(sems["cc"], lyr)
            dmai(sy, cc_in.ap(), s_ep[:], slow=True)
        ld_e = cnt["ld"]

        @block.gpsimd
        def _(g, ld_e=ld_e):
            g.wait_ge(sems["ld"], ld_e)
            inc(g.collective_compute(
                "AllGather", ALU.bypass,
                replica_groups=[[0, 1, 2, 3], [4, 5, 6, 7]],
                ins=[cc_in.ap().opt()], outs=[cc_out.ap().opt()]), "cc")
        cc_n = cnt["cc"]

        @block.sync
        def _(sy, cc_n=cc_n):
            sy.wait_ge(sems["cc"], cc_n)
            dmai(sy, s_eg[:].rearrange("p (j c) -> p j c", c=8),
                 cc_out.ap().rearrange("(j p) c -> p j c", p=128))
        ld_eg = cnt["ld"]

        tab_pass(True, pre_wait=("ld", ld_eg))

        @block.vector
        def _(v, lyr=lyr):
            v = _DrainV(v)
            for dt_ in range(NKD):
                v.tensor_scalar(out=tcol(dt_, 53), in0=tcol(dt_, 42), scalar1=tcol(dt_, 42),
                                scalar2=None, op0=ALU.mult)
            v.drain()
            for half in range(2):
                for dt_ in range(NKD):
                    cofs = half * 4 + dt_
                    E = [s_eg[:, j * 8 + cofs:j * 8 + cofs + 1] for j in range(4)]
                    q = dt_ * 16
                    cd = s_cand[:, q:q + 4]
                    v.memset(cd[:, 0:1], 0.0)
                    v.tensor_copy(out=cd[:, 1:2], in_=E[0])
                    v.scalar_tensor_tensor(out=cd[:, 2:3], in0=E[0], scalar=tcol(dt_, 42),
                                           in1=E[1], op0=ALU.mult, op1=ALU.add)
                    v.scalar_tensor_tensor(out=cd[:, 3:4], in0=E[1], scalar=tcol(dt_, 42),
                                           in1=E[2], op0=ALU.mult, op1=ALU.add)
                v.drain()
                for dt_ in range(NKD):
                    q = dt_ * 16
                    E0 = s_eg[:, 0 * 8 + half * 4 + dt_:0 * 8 + half * 4 + dt_ + 1]
                    v.scalar_tensor_tensor(out=s_cand[:, q + 3:q + 4], in0=E0,
                                           scalar=tcol(dt_, 53), in1=s_cand[:, q + 3:q + 4],
                                           op0=ALU.mult, op1=ALU.add)
                v.drain()
                for dt_ in range(NKD):
                    q = dt_ * 16
                    v.tensor_tensor(out=s_cand[:, q:q + 4], in0=s_cand[:, q:q + 4],
                                    in1=s_selv[:], op=ALU.mult)
                v.drain()
                for dt_ in range(NKD):
                    q = dt_ * 16
                    v.tensor_reduce(out=tcol(dt_, 51 + half), in_=s_cand[:, q:q + 4],
                                    axis=AXF, op=ALU.add)
                v.drain()
            for dt_ in range(NKD):
                v.tensor_scalar(out=tcol(dt_, 54), in0=tcol(dt_, 49), scalar1=tcol(dt_, 46),
                                scalar2=None, op0=ALU.mult)
                v.tensor_scalar(out=tcol(dt_, 55), in0=tcol(dt_, 50), scalar1=tcol(dt_, 46),
                                scalar2=None, op0=ALU.mult)
            v.drain()
            for dt_ in range(NKD):
                v.tensor_tensor(out=tcol(dt_, 54), in0=tcol(dt_, 51), in1=tcol(dt_, 54),
                                op=ALU.add)
                v.tensor_tensor(out=tcol(dt_, 55), in0=tcol(dt_, 52), in1=tcol(dt_, 55),
                                op=ALU.subtract)
            v.drain()
            for dt_ in range(NKD):
                v.tensor_scalar(out=tcol(dt_, 54), in0=tcol(dt_, 54), scalar1=tcol(dt_, 43),
                                scalar2=None, op0=ALU.mult)
                ins = v.tensor_scalar(out=tcol(dt_, 55), in0=tcol(dt_, 55), scalar1=tcol(dt_, 43),
                                      scalar2=None, op0=ALU.mult)
            inc(ins, "ve")

        for dt_ in range(NKD):
            @block.vector
            def _(v, dt_=dt_, lyr=lyr):
                v.scalar_tensor_tensor(out=s_sa[:, dt_ * T:(dt_ + 1) * T], in0=rt(dt_),
                                       scalar=tcol(dt_, 54),
                                       in1=s_sa[:, dt_ * T:(dt_ + 1) * T],
                                       op0=ALU.mult, op1=ALU.add)
                v.scalar_tensor_tensor(out=s_s2[:, dt_ * T:(dt_ + 1) * T], in0=rt(dt_),
                                       scalar=tcol(dt_, 55),
                                       in1=s_s2[:, dt_ * T:(dt_ + 1) * T],
                                       op0=ALU.mult, op1=ALU.add)
                v.drain()
                v.tensor_tensor(out=s_wa[:], in0=cgt(dt_), in1=s_sa[:, dt_ * T:(dt_ + 1) * T],
                                op=ALU.mult)
                v.tensor_tensor(out=s_wb[:], in0=sgt(dt_), in1=s_s2[:, dt_ * T:(dt_ + 1) * T],
                                op=ALU.mult)
                v.drain()
                v.tensor_tensor(out=s_wa[:], in0=s_wa[:], in1=s_wb[:], op=ALU.add)
                v.drain()
                ins = v.scalar_tensor_tensor(out=s_h[:, dt_ * T:(dt_ + 1) * T], in0=s_wa[:],
                                             scalar=par("scl", dt_, lyr),
                                             in1=s_h[:, dt_ * T:(dt_ + 1) * T],
                                             op0=ALU.mult, op1=ALU.add)
                inc(ins, "ve")
        ve_hmid = cnt["ve"]

        if dbg and lyr == 0:
            @block.sync
            def _(sy):
                sy.wait_ge(sems["ve"], ve_hmid)
                dmai(sy, dbg_out["dbg_hd0"], s_h[:], "st")


            @block.vector
            def _(v):
                v.wait_ge(sems["st"], cnt["st"])
        @block.scalar
        def _(s, ve_hmid=ve_hmid):
            s.wait_ge(sems["ve"], ve_hmid)
            inc(s.activation(out=s_h2f[:], in_=s_h[:], func=AF.Square), "sc")
        ve_rows2 = ln_stats_rows(s_h, s_h2f, NKD, DIM, [("sc", cnt["sc"])])
        sc_ybf = ln_norm_affine(s_y, AF.Identity, "ln2w", "ln2b", lyr, ve_rows2)

        for n in range(2):
            pe_prev = cnt["pe"]

            @block.sync
            def _(sy, lyr=lyr, n=n, pe_prev=pe_prev):
                if lyr == 0 and n == 0:
                    sy.wait_ge(sems["pe"], pe_s0)
                else:
                    sy.wait_ge(sems["pe"], pe_prev)
                for q in range(2):
                    dmai(sy, s_wf[:, q * 4096:(q + 1) * 4096],
                         dr["w1T"][lyr][:, q * 4096:(q + 1) * 4096])
            ld_w1 = cnt["ld"]
            pe_w1_0 = cnt["pe"]
            sc_y2_0 = cnt["sc"]

            @block.tensor
            def _(t, n=n, lyr=lyr, pe_w1_0=pe_w1_0, sc_y2_0=sc_y2_0, ld_w1=ld_w1):
                t.wait_ge(sems["sc"], sc_ybf)
                t.wait_ge(sems["ld"], ld_w1)
                for m in range(NKF):
                    if m >= 2:
                        t.wait_ge(sems["sc"], sc_y2_0 + m - 1)
                    else:
                        t.wait_ge(sems["sc"], sc_y2_0)
                    for k in range(NKD):
                        ins = t.matmul(P[m % 2][:],
                                       s_wf[:, k * DFF + m * 128:k * DFF + m * 128 + 128],
                                       s_y[:, k * T + n * TH:k * T + n * TH + TH],
                                       start=(k == 0), stop=(k == NKD - 1))
                    inc(ins, "pe")
            pe_w1e = cnt["pe"]

            @block.scalar
            def _(s, n=n, lyr=lyr, pe_w1_0=pe_w1_0):
                for m in range(NKF):
                    s.wait_ge(sems["pe"], pe_w1_0 + m + 1)
                    bc = (m * DEPTH) + lyr
                    inc(s.activation(out=s_scr[:, m * TH:(m + 1) * TH], in_=P[m % 2][:],
                                     func=AF.Silu, bias=s_b1[:, bc:bc + 1], scale=1.0), "sc")
            sc_y2 = cnt["sc"]

            @block.sync
            def _(sy, lyr=lyr, pe_w1e=pe_w1e):
                sy.wait_ge(sems["pe"], pe_w1e)
                for q in range(2):
                    dmai(sy, s_wf[:, q * 4096:(q + 1) * 4096],
                         dr["w2T"][lyr][:, q * 4096:(q + 1) * 4096])
            ld_w2 = cnt["ld"]
            pe_w2_0 = cnt["pe"]
            ve_h_0 = cnt["ve"]

            @block.tensor
            def _(t, n=n, sc_y2=sc_y2, pe_w2_0=pe_w2_0, ve_h_0=ve_h_0, ld_w2=ld_w2):
                t.wait_ge(sems["sc"], sc_y2)
                t.wait_ge(sems["ld"], ld_w2)
                for m in range(NKD):
                    if m >= 2:
                        t.wait_ge(sems["ve"], ve_h_0 + m - 1)
                    else:
                        t.wait_ge(sems["ve"], ve_h_0)
                    for k in range(NKF):
                        ins = t.matmul(P[4 + m % 2][:],
                                       s_wf[:, k * DIM + m * 128:k * DIM + m * 128 + 128],
                                       s_scr[:, k * TH:(k + 1) * TH],
                                       start=(k == 0), stop=(k == NKF - 1))
                    inc(ins, "pe")

            @block.vector
            def _(v, n=n, lyr=lyr, pe_w2_0=pe_w2_0):
                for m in range(NKD):
                    v.wait_ge(sems["pe"], pe_w2_0 + m + 1)
                    bc = (m * DEPTH) + lyr
                    inc(v.scalar_tensor_tensor(
                        out=s_h[:, m * T + n * TH:m * T + n * TH + TH],
                        in0=P[4 + m % 2][:], scalar=s_b2[:, bc:bc + 1],
                        in1=s_h[:, m * T + n * TH:m * T + n * TH + TH],
                        op0=ALU.add, op1=ALU.add), "ve")
        pe_ffn = cnt["pe"]
        barrier(f"d{lyr}")

    if dbg:
        @block.sync
        def _(sy):
            sy.wait_ge(sems["ve"], cnt["ve"])
            dmai(sy, dbg_out["dbg_h"], s_h[:], "st")


        @block.vector
        def _(v):
            v.wait_ge(sems["st"], cnt["st"])
    @block.scalar
    def _(s):
        s.wait_ge(sems["ve"], cnt["ve"])
        inc(s.activation(out=s_h2f[:], in_=s_h[:], func=AF.Square), "sc")
    ve_rows3 = ln_stats_rows(s_h, s_h2f, NKD, DIM, [("sc", cnt["sc"])])
    sc_lnl = ln_norm_affine(s_ybf, AF.Identity, None, None, 0, ve_rows3, w_is_lnl=True)
    barrier("lnl")
    if dbg:
        @block.sync
        def _(sy):
            sy.wait_ge(sems["sc"], sc_lnl)
            dmai(sy, dbg_out["dbg_y"], s_y[:], "st")


        @block.vector
        def _(v):
            v.wait_ge(sems["st"], cnt["st"])
    with ExitStack() as e3:
        s_xf32 = [s_scr[:, 0:1024], s_scr[:, 1024:2048]]
        s_xr = s_scr[:, 2048:3072]
        s_xrb = [mk(e3, f"s_xrb_{i}", [128, T], BF16) for i in range(2)]
        s_x2b = mk(e3, "s_x2b", [128, T], BF16)

        ld_x0 = cnt["ld"]
        pe_wo_0 = cnt["pe"]
        ve_xr_0 = cnt["ve"]
        sc_xr_0 = cnt["sc"]
        st_xr_0 = cnt["st"]
        pe_p3 = cnt["pe"]
        for m in range(NKV):
            @block.sync
            def _(sy, m=m):
                if m == 0:
                    sy.wait_ge(sems["pe"], pe_p3)
                if m >= 2:
                    sy.wait_ge(sems["ve"], ve_xr_0 + (m - 2) * 2 + 2)
                dmai(sy, s_xf32[m % 2][:], dr["xT"][m * 128:(m + 1) * 128, :])
            for n in range(2):
                i = m * 2 + n

                @block.tensor
                def _(t, m=m, n=n, i=i):
                    if i == 0:
                        t.wait_ge(sems["sc"], sc_lnl)
                        t.wait_ge(sems["lg"], lg_wout)
                    if i >= 2:
                        t.wait_ge(sems["ve"], ve_xr_0 + i - 1)
                    for k in range(NKD):
                        ins = t.matmul(P[i % 2][:],
                                       s_wsh[:, k * V + m * 128:k * V + m * 128 + 128],
                                       s_ybf[:, k * T + n * TH:k * T + n * TH + TH],
                                       start=(k == 0), stop=(k == NKD - 1))
                    inc(ins, "pe")

                @block.vector
                def _(v, m=m, n=n, i=i):
                    v.wait_ge(sems["ld"], ld_x0 + (m + 1) * 16)
                    v.wait_ge(sems["pe"], pe_wo_0 + 2 * i + 1)
                    if i >= 2:
                        v.wait_ge(sems["sc"], sc_xr_0 + (i - 2) * 2 + 2)
                    inc(v.scalar_tensor_tensor(
                        out=s_xr[:, n * TH:n * TH + TH], in0=P[i % 2][:],
                        scalar=s_bout[:, m:m + 1],
                        in1=s_xf32[m % 2][:, n * TH:n * TH + TH],
                        op0=ALU.add, op1=ALU.add), "ve")

                @block.scalar
                def _(s, m=m, n=n, i=i):
                    s.wait_ge(sems["ve"], ve_xr_0 + i + 1)
                    if m >= 2 and n == 0:
                        s.wait_ge(sems["st"], st_xr_0 + (m - 1) * 16)
                    inc(s.activation(out=s_xrb[m % 2][:, n * TH:n * TH + TH],
                                     in_=s_xr[:, n * TH:n * TH + TH], func=AF.Copy), "sc")
                    inc(s.activation(out=s_x2b[:, n * TH:n * TH + TH],
                                     in_=s_xr[:, n * TH:n * TH + TH], func=AF.Square), "sc")

                @block.tensor
                def _(t, m=m, n=n, i=i):
                    t.wait_ge(sems["sc"], sc_xr_0 + i * 2 + 2)
                    pr = P[2] if n == 0 else P[5]
                    t.matmul(pr[0:1, :], s_onc[:],
                             s_xrb[m % 2][:, n * TH:n * TH + TH],
                             start=(m == 0), stop=(m == NKV - 1))
                    inc(t.matmul(pr[32:33, :], s_onc[:],
                                 s_x2b[:, n * TH:n * TH + TH],
                                 start=(m == 0), stop=(m == NKV - 1)), "pe")

            @block.sync
            def _(sy, m=m):
                sy.wait_ge(sems["sc"], sc_xr_0 + m * 4 + 4)
                dmai(sy, xrd.ap()[:, m:m + 1, :].rearrange("p a b -> p (a b)"),
                     s_xrb[m % 2][:], "st")

        pe_allst = cnt["pe"]

        @block.vector
        def _(v):
            v.wait_ge(sems["pe"], pe_allst)
            for n in range(2):
                pr = P[2] if n == 0 else P[5]
                v.tensor_scalar(out=s_row[0:1, n * TH:n * TH + TH],
                                in0=pr[0:1, :], scalar1=1.0 / V,
                                scalar2=None, op0=ALU.mult)
                v.tensor_scalar(out=s_row[0:1, 1024 + n * TH:1024 + n * TH + TH],
                                in0=pr[32:33, :], scalar1=1.0 / V,
                                scalar2=None, op0=ALU.mult)
            v.drain()
            for n in range(2):
                v.tensor_tensor(out=s_row[0:1, 2048:2048 + TH],
                                in0=s_row[0:1, n * TH:n * TH + TH],
                                in1=s_row[0:1, n * TH:n * TH + TH], op=ALU.mult)
                v.drain()
                ins = v.tensor_tensor(out=s_row[0:1, 1024 + n * TH:1024 + n * TH + TH],
                                      in0=s_row[0:1, 1024 + n * TH:1024 + n * TH + TH],
                                      in1=s_row[0:1, 2048:2048 + TH],
                                      op=ALU.subtract)
                v.drain()
            inc(ins, "ve")
        ve_v4 = cnt["ve"]

        @block.scalar
        def _(s):
            s.wait_ge(sems["ve"], ve_v4)
            for n in range(2):
                ins = s.activation(out=s_row[0:1, 1024 + n * TH:1024 + n * TH + TH],
                                   in_=s_row[0:1, 1024 + n * TH:1024 + n * TH + TH],
                                   func=AF.Sqrt, bias=s_cst[0:1, 0:1], scale=1.0)
            inc(ins, "sc")
        sc_q4 = cnt["sc"]

        @block.vector
        def _(v):
            v.wait_ge(sems["sc"], sc_q4)
            for n in range(2):
                ins = v.reciprocal(out=s_row[0:1, 1024 + n * TH:1024 + n * TH + TH],
                                   in_=s_row[0:1, 1024 + n * TH:1024 + n * TH + TH])
            inc(ins, "ve")
        ve_rows4 = cnt["ve"]

    e2.close()
    e_w.close()

    e4 = ExitStack()
    s_x1 = mk(e4, "s_x1", [128, NKV * T], BF16)
    s_xg = mk(e4, "s_xg", [32, T], BF16)
    s_xrr = [mk(e4, f"s_xrr_{i}", [128, TH], BF16) for i in range(2)]
    s_qw = [mk(e4, f"s_qw_{i}", [128, NKV * 128], BF16) for i in range(4)]
    s_gb = [mk(e4, f"s_gb_{i}", [128, V], F32) for i in range(4)]
    s_nzt = [mk(e4, f"s_nzt_{i}", [32, 256], BF16) for i in range(2)]
    s_hst = [mk(e4, f"s_hst_{i}", [128, TH], BF16) for i in range(4)]
    s_gt = [mk(e4, f"s_gt_{i}", [128, TH], F32) for i in range(2)]

    for n in range(2):
        pe_bc4 = ln_bcast(n, ve_rows4,
                          extra_waits=([] if n == 0 else [("ve", cnt["ve"])]))
        ld_xr_0 = cnt["ld"]
        ve_x1_0 = cnt["ve"]

        @block.sync
        def _(sy, n=n, ve_x1_0=ve_x1_0):
            sy.wait_ge(sems["st"], st_xr_0 + NKV * 16)
            for m in range(NKV):
                if m >= 2:
                    sy.wait_ge(sems["ve"], ve_x1_0 + (m - 2) * 2 + 2)
                else:
                    sy.wait_ge(sems["ve"], ve_x1_0)
                dmai(sy, s_xrr[m % 2][:],
                     xrd.ap()[:, m:m + 1, n * TH:n * TH + TH].rearrange("p a b -> p (a b)"))

        @block.vector
        def _(v, n=n, pe_bc4=pe_bc4, ld_xr_0=ld_xr_0):
            v.wait_ge(sems["pe"], pe_bc4)
            for m in range(NKV):
                v.wait_ge(sems["ld"], ld_xr_0 + (m + 1) * 16)
                sl = slice(m * T + n * TH, m * T + n * TH + TH)
                v.tensor_tensor(out=s_x1[:, sl], in0=s_xrr[m % 2][:], in1=P[3][:],
                                op=ALU.subtract)
                v.drain()
                inc(v.tensor_tensor(out=s_x1[:, sl], in0=s_x1[:, sl], in1=P[4][:],
                                    op=ALU.mult), "ve", 2)
    ve_x1 = cnt["ve"]

    @block.sync
    def _(sy):
        sy.wait_ge(sems["ve"], ve_x1)
        dmai(sy, x1d.ap().rearrange("p a b -> p (a b)"), s_x1[:], "st")
    st_x1 = cnt["st"]
    barrier("x1")

    if dbg:
        @block.scalar
        def _(s):
            s.wait_ge(sems["ve"], ve_x1)
            inc(s.activation(out=s_sa[:, 0:NKD * T] if False else s_h[:], in_=s_x1[:, 0:NKD * T],
                             func=AF.Copy), "sc")

        @block.sync
        def _(sy):
            sy.wait_ge(sems["sc"], cnt["sc"])
            dmai(sy, dbg_out["dbg_x1"][:, 0:NKD * T], s_h[:], "st")


        @block.vector
        def _(v):
            v.wait_ge(sems["st"], cnt["st"])
    sc_xg_0 = cnt["sc"]
    for n in range(2):
        @block.tensor
        def _(t, n=n):
            t.wait_ge(sems["ve"], ve_x1)
            if n == 1:
                t.wait_ge(sems["sc"], sc_xg_0 + 1)
            for k in range(NKV):
                ins = t.matmul(P[6][0:32, :], s_e32[:, k * 32:(k + 1) * 32],
                               s_x1[:, k * T + n * TH:k * T + n * TH + TH],
                               start=(k == 0), stop=(k == NKV - 1))
            inc(ins, "pe")
        pe_xg = cnt["pe"]

        @block.scalar
        def _(s, n=n, pe_xg=pe_xg):
            s.wait_ge(sems["pe"], pe_xg)
            inc(s.activation(out=s_xg[0:32, n * TH:n * TH + TH], in_=P[6][0:32, :],
                             func=AF.Copy), "sc")
    sc_xgv = cnt["sc"]
    barrier("xg")

    PW = 4
    SW = 2
    GW = 32
    LW = 64
    TW = 32
    pe_g0, ve_g0, sc_g0, lg_g0, ld_g0, st_g0 = (cnt["pe"], cnt["ve"], cnt["sc"],
                                                cnt["lg"], cnt["ld"], cnt["st"])

    @block.gpsimd
    def _(g):
        g.wait_ge(sems["ve"], ve_x1)
        for w in range(NKH):
            if w >= 2:
                g.wait_ge(sems["pe"], pe_g0 + (w - 1) * PW)
            dmai(g, s_qw[(w % 2) * 2][:], dr["qg"][:, w:w + 1, :].rearrange("p a b -> p (a b)"), "lg")
            dmai(g, s_qw[(w % 2) * 2 + 1][:], dr["qu"][:, w:w + 1, :].rearrange("p a b -> p (a b)"), "lg")

    dq_end = [0] * NKH
    ev_at = [[0, 0] for _ in range(NKH)]

    def emit_evict(v, w):
        v.wait_ge(sems["pe"], pe_g0 + w * PW + PW)
        v.wait_ge(sems["sc"], sc_g0 + w * SW + SW)
        if w >= 2:
            v.wait_ge(sems["st"], st_g0 + (w - 1) * TW)
        for n in range(2):
            ev_at[w][n] = inc(v.tensor_tensor(out=s_hst[(w % 2) * 2 + n][:], in0=s_gt[n][:],
                                              in1=P[4 + n][:], op=ALU.mult), "ve")

    @block.vector
    def _(v):
        for w in range(NKH):
            v.wait_ge(sems["ld"], ld_g0 + (w + 1) * LW)
            v.wait_ge(sems["lg"], lg_g0 + (w + 1) * GW)
            for wt in range(2):
                qwb = s_qw[(w % 2) * 2 + wt]
                gbb = s_gb[(w % 2) * 2 + wt]
                for b in range(8):
                    dq_end[w] = inc(v.tensor_tensor(out=qwb[:, b * 512:(b + 1) * 512],
                                                    in0=qwb[:, b * 512:(b + 1) * 512],
                                                    in1=gbb[:, b * 512:(b + 1) * 512],
                                                    op=ALU.mult), "ve")
            if w >= 1:
                emit_evict(v, w - 1)
        emit_evict(v, NKH - 1)

    @block.scalar
    def _(s):
        for w in range(NKH):
            for n in range(2):
                s.wait_ge(sems["pe"], pe_g0 + w * PW + n + 1)
                if w >= 1:
                    s.wait_ge(sems["ve"], ev_at[w - 1][1])
                inc(s.activation(out=s_gt[n][:], in_=P[2 + n][:], func=AF.Silu), "sc")

    def g_spill(sy, w):
        for n in range(2):
            sy.wait_ge(sems["ve"], ev_at[w][n])
            dmai(sy, hsp[n].ap()[:, w:w + 1, :].rearrange("p a b -> p (a b)"),
                 s_hst[(w % 2) * 2 + n][:], "st")

    @block.sync
    def _(sy):
        for w in range(NKH):
            if w >= 2:
                g_spill(sy, w - 2)
                sy.wait_ge(sems["pe"], pe_g0 + (w - 1) * PW)
            dmai(sy, s_nzt[w % 2][:, 0:128], nzg_d.ap()[:, w * 128:(w + 1) * 128])
            dmai(sy, s_nzt[w % 2][:, 128:256], nzu_d.ap()[:, w * 128:(w + 1) * 128])
            if w >= 2:
                sy.wait_ge(sems["ve"], dq_end[w - 2])
            dmai(sy, s_gb[(w % 2) * 2][:], dr["sgw"][w:w + 1, :].broadcast_to((128, V)))
            dmai(sy, s_gb[(w % 2) * 2 + 1][:], dr["suw"][w:w + 1, :].broadcast_to((128, V)))
        g_spill(sy, NKH - 2)
        g_spill(sy, NKH - 1)

    @block.tensor
    def _(t):
        t.wait_ge(sems["sc"], sc_xgv)
        t.wait_ge(sems["st"], st_x1)
        for w in range(NKH):
            t.wait_ge(sems["lg"], lg_g0 + (w + 1) * GW)
            t.wait_ge(sems["ld"], ld_g0 + (w + 1) * LW)
            t.wait_ge(sems["ve"], dq_end[w])
            if w >= 1:
                t.wait_ge(sems["sc"], sc_g0 + w * SW)
                t.wait_ge(sems["ve"], ev_at[w - 1][1])
            for wt in range(2):
                qwb = s_qw[(w % 2) * 2 + wt]
                for n in range(2):
                    pb = P[2 + wt * 2 + n]
                    for k in range(NKV):
                        t.matmul(pb[:], qwb[:, k * 128:(k + 1) * 128],
                                 s_x1[:, k * T + n * TH:k * T + n * TH + TH],
                                 start=(k == 0), stop=False)
                    inc(t.matmul(pb[:], s_nzt[w % 2][0:32, wt * 128:wt * 128 + 128],
                                 s_xg[0:32, n * TH:n * TH + TH], start=False, stop=True), "pe")

    st_hsp = cnt["st"]
    pe_gend = cnt["pe"]
    barrier("g")

    e4.close()
    e5 = ExitStack()
    s_hh = mk(e5, "s_hh", [128, NKH * TH], BF16)
    s_qd = [mk(e5, f"s_qd_{i}", [128, NKH * 128], BF16) for i in range(2)]
    s_drow = [mk(e5, "s_drow_0", [1, HD], BF16)] * 2
    s_e86 = mk(e5, "s_e86", [128, NKH * NKH], BF16)
    s_hg = mk(e5, "s_hg", [NKH, TH], BF16)
    s_nzdt = [mk(e5, f"s_nzdt_{i}", [NKH, 128], BF16) for i in range(2)]
    s_xtt = [mk(e5, f"s_xtt_{i}", [128, TH], BF16) for i in range(2)]
    s_xfb = [mk(e5, f"s_xfb_{i}", [128, TH], BF16) for i in range(2)]
    s_sq = mk(e5, "s_sq", [128, TH], BF16)
    s_of = [mk(e5, f"s_of_{i}", [128, TH], F32) for i in range(2)]

    @block.gpsimd
    def _(g):
        g.wait_ge(sems["pe"], pe_gend)
        g.wait_ge(sems["st"], st_hsp)
        dmai(g, s_e86[:], dr["e86"], "lg")
    lg_d0p = cnt["lg"]

    for hf in range(2):
        @block.sync
        def _(sy, hf=hf):
            sy.wait_ge(sems["st"], st_hsp)
            if hf == 1:
                sy.wait_ge(sems["pe"], cnt["pe"])
            for q in range(2):
                dmai(sy, s_hh[:, q * 43 * TH:(q + 1) * 43 * TH],
                     hsp[hf].ap().rearrange("p a b -> p (a b)")[:, q * 43 * TH:(q + 1) * 43 * TH])
        ld_hh = cnt["ld"]

        @block.tensor
        def _(t, hf=hf, ld_hh=ld_hh):
            t.wait_ge(sems["ld"], ld_hh)
            if hf == 0:
                t.wait_ge(sems["lg"], lg_d0p)
            for k in range(NKH):
                ins = t.matmul(P[4][0:NKH, :], s_e86[:, k * NKH:(k + 1) * NKH],
                               s_hh[:, k * TH:(k + 1) * TH],
                               start=(k == 0), stop=(k == NKH - 1))
            inc(ins, "pe")
        pe_hg = cnt["pe"]

        @block.scalar
        def _(s, pe_hg=pe_hg):
            s.wait_ge(sems["pe"], pe_hg)
            inc(s.activation(out=s_hg[:], in_=P[4][0:NKH, :], func=AF.Copy), "sc")
        sc_hg = cnt["sc"]

        pe_d0, ve_d0, sc_d0, lg_d0, ld_d0, st_d0 = (cnt["pe"], cnt["ve"], cnt["sc"],
                                                    cnt["lg"], cnt["ld"], cnt["st"])
        pe_sb = [[0] * DBLK for _ in range(NKV)]
        pe_out = [0] * NKV
        pe_st = [0] * NKV
        c = pe_d0
        for b in range(DBLK):
            c += 1
            pe_sb[0][b] = c
        for i in range(NKV):
            if i + 1 < NKV:
                for b in range(DBLK):
                    c += 1
                    pe_sb[i + 1][b] = c
            c += 1
            pe_out[i] = c
            if i >= 1:
                c += 1
                pe_st[i - 1] = c
        c += 1
        pe_st[NKV - 1] = c
        dq_at = [[0] * DBLK for _ in range(NKV)]
        ev2 = [0] * NKV
        c = ve_d0
        for m in range(NKV):
            for b in range(DBLK):
                c += 1
                dq_at[m][b] = c
            if m >= 1:
                c += 1
                ev2[m - 1] = c
        c += 1
        ev2[NKV - 1] = c

        @block.gpsimd
        def _(g, hf=hf, pe_d0=pe_d0, pe_sb=pe_sb, pe_out=pe_out):
            for m in range(NKV):
                if m == 0:
                    g.wait_ge(sems["pe"], pe_d0)
                else:
                    g.wait_ge(sems["pe"], pe_sb[m - 1][DBLK - 1])
                    if m >= 2:
                        g.wait_ge(sems["pe"], pe_out[m - 2])
                dmai(g, s_drow[m % 2][:], dr["sdw"][m:m + 1, :], "lg")
                dmai(g, s_qd[m % 2][:], dr["qd"][:, m:m + 1, :].rearrange("p a b -> p (a b)"), "lg")

        def d_spill(sy, m, hf):
            sy.wait_ge(sems["pe"], pe_st[m])
            dmai(sy, xfd[hf].ap()[:, m:m + 1, :].rearrange("p a b -> p (a b)"),
                 s_xfb[m % 2][:], "st")

        @block.sync
        def _(sy, hf=hf):
            for m in range(NKV):
                if m >= 2:
                    d_spill(sy, m - 2, hf)
                    sy.wait_ge(sems["pe"], pe_out[m - 2])
                dmai(sy, s_nzdt[m % 2][:], nzd_d.ap()[:, m * 128:(m + 1) * 128])
                dmai(sy, s_xtt[m % 2][:],
                     x1d.ap()[:, m:m + 1, hf * TH:hf * TH + TH].rearrange("p a b -> p (a b)"))
            d_spill(sy, NKV - 2, hf)
            d_spill(sy, NKV - 1, hf)

        @block.vector
        def _(v, hf=hf, ld_d0=ld_d0, st_d0=st_d0):
            def evict(m):
                v.wait_ge(sems["pe"], pe_out[m])
                v.wait_ge(sems["ld"], ld_d0 + (m + 1) * 32)
                if m >= 2:
                    v.wait_ge(sems["st"], st_d0 + (m - 1) * 16)
                inc(v.tensor_tensor(out=s_xfb[m % 2][:], in0=P[2 + m % 2][:],
                                    in1=s_xtt[m % 2][:], op=ALU.add), "ve")
            for m in range(NKV):
                v.wait_ge(sems["lg"], lg_d0 + (m + 1) * 32)
                for b in range(DBLK):
                    v.wait_ge(sems["pe"], pe_sb[m][b])
                    nn = min(512, HD - b * 512)
                    inc(v.tensor_tensor(out=s_qd[m % 2][:, b * 512:b * 512 + nn],
                                        in0=s_qd[m % 2][:, b * 512:b * 512 + nn],
                                        in1=P[b % 2][0:128, 0:nn], op=ALU.mult), "ve")
                if m >= 1:
                    evict(m - 1)
            evict(NKV - 1)

        @block.scalar
        def _(s, hf=hf):
            for m in range(NKV):
                s.wait_ge(sems["ve"], ev2[m])
                inc(s.activation(out=s_sq[:], in_=s_xfb[m % 2][:], func=AF.Square), "sc")

        @block.tensor
        def _(t, hf=hf, sc_hg=sc_hg, sc_d0=sc_d0, lg_d0=lg_d0, ld_d0=ld_d0):
            t.wait_ge(sems["sc"], sc_hg)

            def sb_mms(m):
                t.wait_ge(sems["lg"], lg_d0 + m * 32 + 16)
                for b in range(DBLK):
                    blk = m * DBLK + b
                    if blk >= 2:
                        t.wait_ge(sems["ve"], dq_at[(blk - 2) // DBLK][(blk - 2) % DBLK])
                    nn = min(512, HD - b * 512)
                    inc(t.matmul(P[b % 2][0:128, 0:nn], s_onrb[:],
                                 s_drow[m % 2][0:1, b * 512:b * 512 + nn],
                                 start=True, stop=True), "pe")

            sb_mms(0)
            for m in range(NKV):
                if m + 1 < NKV:
                    sb_mms(m + 1)
                t.wait_ge(sems["ld"], ld_d0 + (m + 1) * 32)
                t.wait_ge(sems["ve"], dq_at[m][DBLK - 1])
                if m >= 2:
                    t.wait_ge(sems["ve"], ev2[m - 2])
                pb = P[2 + m % 2]
                for k in range(NKH):
                    t.matmul(pb[:], s_qd[m % 2][:, k * 128:(k + 1) * 128],
                             s_hh[:, k * TH:(k + 1) * TH],
                             start=(k == 0), stop=False)
                inc(t.matmul(pb[:], s_nzdt[m % 2][:], s_hg[:],
                             start=False, stop=True), "pe")
                if m >= 1:
                    stats_of = m - 1
                    t.wait_ge(sems["ve"], ev2[stats_of])
                    t.wait_ge(sems["sc"], sc_d0 + stats_of + 1)
                    t.matmul(P[5][0:1, :], s_onc[:], s_xfb[stats_of % 2][:],
                             start=(stats_of == 0), stop=False)
                    inc(t.matmul(P[5][32:33, :], s_onc[:], s_sq[:],
                                 start=(stats_of == 0), stop=False), "pe")
            t.wait_ge(sems["ve"], ev2[NKV - 1])
            t.wait_ge(sems["sc"], sc_d0 + NKV)
            t.matmul(P[5][0:1, :], s_onc[:], s_xfb[(NKV - 1) % 2][:],
                     start=False, stop=True)
            inc(t.matmul(P[5][32:33, :], s_onc[:], s_sq[:],
                         start=False, stop=True), "pe")

        pe_dend = cnt["pe"]

        @block.vector
        def _(v, pe_dend=pe_dend):
            v.wait_ge(sems["pe"], pe_dend)
            v.tensor_scalar(out=s_row[0:1, 0:TH], in0=P[5][0:1, :], scalar1=1.0 / V,
                            scalar2=None, op0=ALU.mult)
            v.tensor_scalar(out=s_row[0:1, 1024:1024 + TH], in0=P[5][32:33, :],
                            scalar1=1.0 / V, scalar2=None, op0=ALU.mult)
            v.drain()
            v.tensor_tensor(out=s_row[0:1, 2048:2048 + TH], in0=s_row[0:1, 0:TH],
                            in1=s_row[0:1, 0:TH], op=ALU.mult)
            v.drain()
            inc(v.tensor_tensor(out=s_row[0:1, 1024:1024 + TH],
                                in0=s_row[0:1, 1024:1024 + TH],
                                in1=s_row[0:1, 2048:2048 + TH], op=ALU.subtract), "ve")
        ve_r5 = cnt["ve"]

        @block.scalar
        def _(s, ve_r5=ve_r5):
            s.wait_ge(sems["ve"], ve_r5)
            inc(s.activation(out=s_row[0:1, 1024:1024 + TH], in_=s_row[0:1, 1024:1024 + TH],
                             func=AF.Sqrt, bias=s_cst[0:1, 0:1], scale=1.0), "sc")
        sc_r5 = cnt["sc"]

        @block.vector
        def _(v, sc_r5=sc_r5):
            v.wait_ge(sems["sc"], sc_r5)
            inc(v.reciprocal(out=s_row[0:1, 1024:1024 + TH],
                             in_=s_row[0:1, 1024:1024 + TH]), "ve")
        ve_r5b = cnt["ve"]

        @block.tensor
        def _(t, ve_r5b=ve_r5b):
            t.wait_ge(sems["ve"], ve_r5b)
            t.matmul(P[6][:], s_onrf[:], s_row[0:1, 0:TH], start=True, stop=True)
            inc(t.matmul(P[7][:], s_onrf[:], s_row[0:1, 1024:1024 + TH],
                         start=True, stop=True), "pe")
        pe_bc5 = cnt["pe"]
        ld_f0 = cnt["ld"]
        ve_f0 = cnt["ve"]
        st_f0 = cnt["st"]

        def f_store(sy, m, hf):
            sy.wait_ge(sems["ve"], ve_f0 + m * 2 + 2)
            dmai(sy, outT[m * 128:(m + 1) * 128, hf * TH:hf * TH + TH],
                 s_of[m % 2][:], "st")

        @block.sync
        def _(sy, hf=hf, ve_f0=ve_f0, st_d0=st_d0):
            sy.wait_ge(sems["st"], st_d0 + NKV * 16)
            for m in range(NKV):
                if m >= 2:
                    f_store(sy, m - 2, hf)
                    sy.wait_ge(sems["ve"], ve_f0 + (m - 2) * 2 + 2)
                dmai(sy, s_xfb[m % 2][:],
                     xfd[hf].ap()[:, m:m + 1, :].rearrange("p a b -> p (a b)"))
            f_store(sy, NKV - 2, hf)
            f_store(sy, NKV - 1, hf)

        @block.vector
        def _(v, hf=hf, pe_bc5=pe_bc5, ld_f0=ld_f0, st_f0=st_f0):
            v.wait_ge(sems["pe"], pe_bc5)
            for m in range(NKV):
                v.wait_ge(sems["ld"], ld_f0 + (m + 1) * 16)
                if m >= 2:
                    v.wait_ge(sems["st"], st_f0 + (m - 1) * 16)
                v.tensor_tensor(out=s_of[m % 2][:], in0=s_xfb[m % 2][:], in1=P[6][:],
                                op=ALU.subtract)
                v.drain()
                inc(v.tensor_tensor(out=s_of[m % 2][:], in0=s_of[m % 2][:], in1=P[7][:],
                                    op=ALU.mult), "ve", 2)

        barrier(f"hf{hf}")

    @block.sync
    def _(sy):
        sy.wait_ge(sems["st"], cnt["st"])

    e5.close()
    es.close()
    return nc


def _stage(inputs, core):
    b, ch = core // 4, core % 4
    ip = {}
    x = inputs["L0_attn_x"][b]
    xT = np.ascontiguousarray(x.T[:, ch * T:(ch + 1) * T])
    ip["xT"] = xT
    sel = np.zeros((128, 4), np.float32)
    sel[:, ch] = 1.0
    ip["selv"] = sel
    Win = inputs["L0_attn_Win"]
    ip["winT"] = np.ascontiguousarray(
        Win.T.reshape(NKV, 128, DIM).transpose(1, 0, 2).reshape(128, NKV * DIM))
    W1 = inputs["L0_attn_W1"]
    ip["w1T"] = np.ascontiguousarray(
        W1.transpose(0, 2, 1).reshape(DEPTH, NKD, 128, DFF).transpose(0, 2, 1, 3)
        .reshape(DEPTH, 128, NKD * DFF))
    W2 = inputs["L0_attn_W2"]
    ip["w2T"] = np.ascontiguousarray(
        W2.transpose(0, 2, 1).reshape(DEPTH, NKF, 128, DIM).transpose(0, 2, 1, 3)
        .reshape(DEPTH, 128, NKF * DIM))
    Wout = inputs["L0_attn_Wout"]
    ip["woutT"] = np.ascontiguousarray(
        Wout.T.reshape(NKD, 128, V).transpose(1, 0, 2).reshape(128, NKD * V))
    pars = []
    for nm, key in [("ln1w", "L0_attn_ln1_w"), ("ln1b", "L0_attn_ln1_b"),
                    ("ln2w", "L0_attn_ln2_w"), ("ln2b", "L0_attn_ln2_b"),
                    ("pre", "L0_attn_phz_re"), ("pim", "L0_attn_phz_im"),
                    ("lre", "L0_attn_lc_re"), ("lim", "L0_attn_lc_im"),
                    ("scl", "L0_attn_sc_lin")]:
        a = inputs[key]
        pars.append(a.T.reshape(NKD, 128, DEPTH).transpose(1, 0, 2))
    ip["par"] = np.ascontiguousarray(np.stack(pars, axis=1).reshape(128, 9 * NKD * DEPTH))
    lnl = np.stack([inputs["L0_attn_lnl_w"], inputs["L0_attn_lnl_b"]], 0)
    ip["lnl"] = np.ascontiguousarray(
        lnl.reshape(2, NKD, 128).transpose(2, 0, 1).reshape(128, 2 * NKD))
    ip["bin"] = np.ascontiguousarray(inputs["L0_attn_bin"].reshape(NKD, 128).T)
    ip["bout"] = np.ascontiguousarray(inputs["L0_attn_bout"].reshape(NKV, 128).T)
    ip["b1"] = np.ascontiguousarray(
        inputs["L0_attn_b1"].T.reshape(NKF, 128, DEPTH).transpose(1, 0, 2)
        .reshape(128, NKF * DEPTH))
    ip["b2"] = np.ascontiguousarray(
        inputs["L0_attn_b2"].T.reshape(NKD, 128, DEPTH).transpose(1, 0, 2)
        .reshape(128, NKD * DEPTH))
    qg = inputs["L0_ffn_qw_gate"]
    ip["qg"] = np.ascontiguousarray(
        qg.reshape(NKV, 128, NKH, 128).transpose(1, 2, 0, 3)
        .reshape(128, NKH, NKV * 128).astype(np.uint8))
    qu = inputs["L0_ffn_qw_up"]
    ip["qu"] = np.ascontiguousarray(
        qu.reshape(NKV, 128, NKH, 128).transpose(1, 2, 0, 3)
        .reshape(128, NKH, NKV * 128).astype(np.uint8))
    qd = inputs["L0_ffn_qw_down"]
    ip["qd"] = np.ascontiguousarray(
        qd.reshape(NKH, 128, NKV, 128).transpose(1, 2, 0, 3)
        .reshape(128, NKV, NKH * 128).astype(np.uint8))
    sgn = inputs["L0_ffn_s_gate"]
    ip["sgw"] = np.ascontiguousarray(
        sgn.reshape(32, NKH, 128).transpose(1, 0, 2).reshape(NKH, V))
    sun = inputs["L0_ffn_s_up"]
    ip["suw"] = np.ascontiguousarray(
        sun.reshape(32, NKH, 128).transpose(1, 0, 2).reshape(NKH, V))
    sdn = inputs["L0_ffn_s_down"]
    ip["sdw"] = np.ascontiguousarray(
        sdn.reshape(NKH, NKV, 128).transpose(1, 0, 2).reshape(NKV, HD))
    ip["zg"] = inputs["L0_ffn_z_gate"].astype(np.uint8)
    ip["zu"] = inputs["L0_ffn_z_up"].astype(np.uint8)
    ip["zd"] = inputs["L0_ffn_z_down"].astype(np.uint8)
    ip["sg"] = sgn
    ip["su"] = sun
    ip["sd"] = sdn
    e32 = np.zeros((32, 128, 32), np.float32)
    for k in range(32):
        e32[k, :, k] = 1.0
    ip["e32"] = np.ascontiguousarray(e32.transpose(1, 0, 2).reshape(128, 32 * 32))
    e86 = np.zeros((NKH, 128, NKH), np.float32)
    for k in range(NKH):
        e86[k, :, k] = 1.0
    ip["e86"] = np.ascontiguousarray(e86.transpose(1, 0, 2).reshape(128, NKH * NKH))
    cst = np.zeros((128, 2), np.float32)
    cst[:, 0] = EPS
    cst[:, 1] = 1e-30
    ip["cst"] = cst
    ip["onc"] = np.ones((128, 1), np.float32)
    ip["onr"] = np.ones((1, 128), np.float32)
    return ip


_NC_CACHE = {}


def kernel(**inputs):
    dbg = bool(inputs.pop("_dbg", False))
    key = dbg
    if key not in _NC_CACHE:
        _NC_CACHE[key] = build_nc(dbg=dbg)
    nc = _NC_CACHE[key]
    in_maps = [_stage(inputs, c) for c in range(8)]
    res = run_bass_kernel_spmd(nc, in_maps, core_ids=list(range(8)))
    out = np.empty((B, L, V), np.float32)
    for c in range(8):
        b, ch = c // 4, c % 4
        out[b, ch * T:(ch + 1) * T, :] = res.results[c]["outT"].T
    if dbg:
        return out, res
    return out


# version 08 (final)
# speedup vs baseline: 1.0934x; 1.0934x over previous
import numpy as np
from contextlib import ExitStack
import concourse.bass as bass
import concourse.mybir as mybir
from concourse.bass_utils import run_bass_kernel_spmd

F32 = mybir.dt.float32
BF16 = mybir.dt.bfloat16
U8 = mybir.dt.uint8
ALU = mybir.AluOpType
AF = mybir.ActivationFunctionType
AXF = mybir.AxisListType.X

B, L, V, DIM, DFF, HD, DEPTH = 2, 4096, 4096, 512, 2048, 11008, 4
T, TH = 1024, 512
NKV, NKD, NKF, NKH = V // 128, DIM // 128, DFF // 128, HD // 128
EPS = 1e-5
PAR_NAMES = ["ln1w", "ln1b", "ln2w", "ln2b", "pre", "pim", "lre", "lim", "scl"]
DBLK = (HD + 511) // 512


STALLS = set()


def build_nc(dbg=False):
    nc = bass.Bass("TRN2", target_bir_lowering=False)
    dr = {}

    def din(name, shape, dt):
        dr[name] = nc.dram_tensor(name, shape, dt, kind="ExternalInput").ap()

    din("xT", [V, T], F32)
    din("selv", [128, 4], F32)
    din("winT", [128, NKV * DIM], F32)
    din("w1T", [DEPTH, 128, NKD * DFF], F32)
    din("w2T", [DEPTH, 128, NKF * DIM], F32)
    din("woutT", [128, NKD * V], F32)
    din("par", [128, 9 * NKD * DEPTH], F32)
    din("lnl", [128, 2 * NKD], F32)
    din("bin", [128, NKD], F32)
    din("bout", [128, NKV], F32)
    din("b1", [128, NKF * DEPTH], F32)
    din("b2", [128, NKD * DEPTH], F32)
    din("qg", [128, NKH, NKV * 128], U8)
    din("qu", [128, NKH, NKV * 128], U8)
    din("qd", [128, NKV, NKH * 128], U8)
    din("sgw", [NKH, V], F32)
    din("suw", [NKH, V], F32)
    din("sdw", [NKV, HD], F32)
    din("zg", [32, HD], U8)
    din("zu", [32, HD], U8)
    din("zd", [NKH, V], U8)
    din("sg", [32, HD], F32)
    din("su", [32, HD], F32)
    din("sd", [NKH, V], F32)
    din("e32", [128, 32 * 32], F32)
    din("e86", [128, NKH * NKH], F32)
    din("cst", [128, 2], F32)
    din("onc", [128, 1], F32)
    din("onr", [1, 128], F32)

    outT = nc.dram_tensor("outT", [V, T], F32, kind="ExternalOutput").ap()
    dbg_out = {}
    if dbg:
        for nm, shp in [("dbg_h", [128, NKD * T]), ("dbg_x1", [128, NKV * T]),
                        ("dbg_y", [128, NKD * T]), ("dbg_sa", [128, NKD * T]),
                        ("dbg_h0", [128, NKD * T]), ("dbg_hd0", [128, NKD * T]),
                        ("dbg_ya", [128, NKD * T]), ("dbg_sc", [128, NKD * T])]:
            dbg_out[nm] = nc.dram_tensor(nm, shp, F32, kind="ExternalOutput").ap()

    hsp = [nc.dram_tensor(f"hsp{i}", [128, NKH, TH], BF16, kind="Internal") for i in range(2)]
    x1d = nc.dram_tensor("x1d", [128, NKV, T], BF16, kind="Internal")
    xrd = nc.dram_tensor("xrd", [128, NKV, T], BF16, kind="Internal")
    xfd = [nc.dram_tensor(f"xfd{i}", [128, NKV, TH], BF16, kind="Internal") for i in range(2)]
    nzg_d = nc.dram_tensor("nzg", [32, HD], BF16, kind="Internal")
    nzu_d = nc.dram_tensor("nzu", [32, HD], BF16, kind="Internal")
    nzd_d = nc.dram_tensor("nzd", [NKH, V], BF16, kind="Internal")
    cc_in = nc.dram_tensor("cc_in", [128, 8], F32, kind="Internal")
    cc_out = nc.dram_tensor("cc_out", [512, 8], F32, kind="Internal")

    es = ExitStack()

    def mk(stack, name, shape, dt):
        return stack.enter_context(nc.sbuf_tensor(name, shape, dt))

    s_par = mk(es, "s_par", [128, 9 * NKD * DEPTH], F32)
    s_lnl = mk(es, "s_lnl", [128, 2 * NKD], F32)
    s_bin = mk(es, "s_bin", [128, NKD], F32)
    s_bout = mk(es, "s_bout", [128, NKV], F32)
    s_b1 = mk(es, "s_b1", [128, NKF * DEPTH], F32)
    s_b2 = mk(es, "s_b2", [128, NKD * DEPTH], F32)
    s_cst = mk(es, "s_cst", [128, 2], F32)
    s_onc = mk(es, "s_onc", [128, 1], BF16)
    s_oncf = mk(es, "s_oncf", [128, 1], F32)
    s_onrb = mk(es, "s_onrb", [1, 128], BF16)
    s_onrf = mk(es, "s_onrf", [1, 128], F32)
    s_selv = mk(es, "s_selv", [128, 4], F32)
    s_e32 = mk(es, "s_e32", [128, 32 * 32], BF16)
    s_row = mk(es, "s_row", [1, 5 * TH], F32)
    s_tiny = mk(es, "s_tiny", [128, 256], F32)
    s_h = mk(es, "s_h", [128, NKD * T], F32)
    e_w = ExitStack()
    s_wsh = mk(e_w, "s_wsh", [128, 16384], BF16)

    P = [es.enter_context(nc.psum_tensor(f"pb{i}", [128, 512], F32)) for i in range(8)]

    sems = {}
    for s in ["ld", "lg", "pe", "ve", "sc", "cc", "st", "lb"]:
        sems[s] = es.enter_context(nc.semaphore(s))
    cnt = {s: 0 for s in sems}
    block = es.enter_context(nc.Block())

    def inc(ins, s, k=1):
        ins.then_inc(sems[s], k)
        cnt[s] += k
        return cnt[s]

    def barrier(name):
        if name not in STALLS:
            return
        vals = dict(cnt)
        for deco in [block.tensor, block.vector, block.scalar, block.sync, block.gpsimd]:
            @deco
            def _(e, vals=vals):
                for s_, v_ in vals.items():
                    e.wait_ge(sems[s_], v_)

    def dmai(eng, out, in_, s="ld", slow=False):
        eng.wait_ge(sems[s], cnt[s])
        return inc(eng.dma_start(out=out, in_=in_,
                                 allow_slow_non_contiguous=slow), s, 16)

    def par(nm, dt_, lyr):
        i = PAR_NAMES.index(nm)
        c = (i * NKD + dt_) * DEPTH + lyr
        return s_par[:, c:c + 1]

    @block.sync
    def _(sy):
        for t_, src in [(s_par, "par"), (s_lnl, "lnl"), (s_bin, "bin"),
                        (s_bout, "bout"), (s_b1, "b1"), (s_b2, "b2"),
                        (s_onrf, "onr"), (s_selv, "selv"), (s_cst, "cst"),
                        (s_oncf, "onc")]:
            dmai(sy, t_[:], dr[src])
    ld_small = cnt["ld"]

    @block.gpsimd
    def _(g):
        dmai(g, s_onc[:], dr["onc"], "lg")
        dmai(g, s_onrb[:], dr["onr"], "lg")
        dmai(g, s_e32[:], dr["e32"], "lg")
    lg_small = cnt["lg"]

    with ExitStack() as e0:
        s_zf = mk(e0, "s_zf", [NKH, HD], F32)
        s_sf = mk(e0, "s_sf", [NKH, HD], F32)
        s_nz = mk(e0, "s_nz", [NKH, HD], BF16)
        for wi, (zn, sn, dst, pp) in enumerate([("zg", "sg", nzg_d, 32),
                                                ("zu", "su", nzu_d, 32),
                                                ("zd", "sd", nzd_d, NKH)]):
            nn = (32 * HD) // pp
            st_pre = cnt["st"]

            @block.gpsimd
            def _(g, zn=zn, wi=wi, pp=pp, nn=nn, st_pre=st_pre):
                if wi > 0:
                    g.wait_ge(sems["st"], st_pre)
                dmai(g, s_zf[0:pp, 0:nn], dr[zn], "lg")
            lg_z = cnt["lg"]

            @block.sync
            def _(sy, sn=sn, wi=wi, pp=pp, nn=nn, st_pre=st_pre):
                if wi > 0:
                    sy.wait_ge(sems["st"], st_pre)
                dmai(sy, s_sf[0:pp, 0:nn], dr[sn])
            ld_z = cnt["ld"]

            @block.vector
            def _(v, pp=pp, nn=nn, lg_z=lg_z, ld_z=ld_z):
                v.wait_ge(sems["lg"], lg_z)
                v.wait_ge(sems["ld"], ld_z)
                inc(v.scalar_tensor_tensor(out=s_nz[0:pp, 0:nn], in0=s_zf[0:pp, 0:nn],
                                           scalar=-1.0, in1=s_sf[0:pp, 0:nn],
                                           op0=ALU.mult, op1=ALU.mult), "ve")
            ve_z = cnt["ve"]

            @block.sync
            def _(sy, dst=dst, pp=pp, nn=nn, ve_z=ve_z):
                sy.wait_ge(sems["ve"], ve_z)
                dmai(sy, dst.ap(), s_nz[0:pp, 0:nn], "st")
    st_nz = cnt["st"]

    def ln_stats_rows(src_b, src_2b, nk, nd, waits):
        @block.tensor
        def _(t):
            t.wait_ge(sems["ld"], ld_small)
            t.wait_ge(sems["lg"], lg_small)
            for s_, v_ in waits:
                t.wait_ge(sems[s_], v_)
            for n in range(2):
                pr = P[2] if n == 0 else P[5]
                for which, src in [(0, src_b), (1, src_2b)]:
                    r0 = which * 32
                    for k in range(nk):
                        ins = t.matmul(pr[r0:r0 + 1, :], s_oncf[:],
                                       src[:, k * T + n * TH:k * T + n * TH + TH],
                                       start=(k == 0), stop=(k == nk - 1))
                    inc(ins, "pe")
        pe_r = cnt["pe"]

        @block.vector
        def _(v):
            v.wait_ge(sems["pe"], pe_r)
            for n in range(2):
                pr = P[2] if n == 0 else P[5]
                v.tensor_scalar(out=s_row[0:1, n * TH:n * TH + TH],
                                in0=pr[0:1, :], scalar1=1.0 / nd,
                                scalar2=None, op0=ALU.mult)
                v.tensor_scalar(out=s_row[0:1, 1024 + n * TH:1024 + n * TH + TH],
                                in0=pr[32:33, :], scalar1=1.0 / nd,
                                scalar2=None, op0=ALU.mult)
            v.drain()
            for n in range(2):
                v.tensor_tensor(out=s_row[0:1, 2048:2048 + TH],
                                in0=s_row[0:1, n * TH:n * TH + TH],
                                in1=s_row[0:1, n * TH:n * TH + TH], op=ALU.mult)
                v.drain()
                ins = v.tensor_tensor(out=s_row[0:1, 1024 + n * TH:1024 + n * TH + TH],
                                      in0=s_row[0:1, 1024 + n * TH:1024 + n * TH + TH],
                                      in1=s_row[0:1, 2048:2048 + TH],
                                      op=ALU.subtract)
                v.drain()
            inc(ins, "ve")
        ve_v = cnt["ve"]

        @block.scalar
        def _(s):
            s.wait_ge(sems["ve"], ve_v)
            for n in range(2):
                ins = s.activation(out=s_row[0:1, 1024 + n * TH:1024 + n * TH + TH],
                                   in_=s_row[0:1, 1024 + n * TH:1024 + n * TH + TH],
                                   func=AF.Sqrt, bias=s_cst[0:1, 0:1], scale=1.0)
            inc(ins, "sc")
        sc_q = cnt["sc"]

        @block.vector
        def _(v):
            v.wait_ge(sems["sc"], sc_q)
            for n in range(2):
                ins = v.reciprocal(out=s_row[0:1, 1024 + n * TH:1024 + n * TH + TH],
                                   in_=s_row[0:1, 1024 + n * TH:1024 + n * TH + TH])
            inc(ins, "ve")
        return cnt["ve"]

    def ln_bcast(n, ve_rows, extra_waits=()):
        @block.tensor
        def _(t):
            t.wait_ge(sems["ld"], ld_small)
            t.wait_ge(sems["ve"], ve_rows)
            for s_, v_ in extra_waits:
                t.wait_ge(sems[s_], v_)
            t.matmul(P[3][:], s_onrf[:], s_row[0:1, n * TH:n * TH + TH],
                     start=True, stop=True)
            inc(t.matmul(P[4][:], s_onrf[:], s_row[0:1, 1024 + n * TH:1024 + n * TH + TH],
                         start=True, stop=True), "pe")
        return cnt["pe"]

    with ExitStack() as e1:
        s_winf = mk(e1, "s_winf", [128, NKV * DIM], F32)
        s_xs = [mk(e1, f"s_xs_{i}", [128, T], F32) for i in range(2)]

        @block.sync
        def _(sy):
            sy.wait_ge(sems["st"], st_nz)
            for q in range(4):
                dmai(sy, s_winf[:, q * 4096:(q + 1) * 4096],
                     dr["winT"][:, q * 4096:(q + 1) * 4096])
        ld_win = cnt["ld"]
        pe_s00 = cnt["pe"]
        ld_xs0 = cnt["ld"]

        @block.sync
        def _(sy):
            for k in range(NKV):
                if k >= 2:
                    sy.wait_ge(sems["pe"], pe_s00 + k - 1)
                dmai(sy, s_xs[k % 2][:], dr["xT"][k * 128:(k + 1) * 128, :])

        @block.tensor
        def _(t):
            t.wait_ge(sems["ld"], ld_win)
            for k in range(NKV):
                t.wait_ge(sems["ld"], ld_xs0 + (k + 1) * 16)
                for m in range(NKD):
                    for n in range(2):
                        ins = t.matmul(P[m * 2 + n][:],
                                       s_winf[:, k * DIM + m * 128:k * DIM + m * 128 + 128],
                                       s_xs[k % 2][:, n * TH:n * TH + TH],
                                       start=(k == 0), stop=(k == NKV - 1))
                inc(ins, "pe")
        pe_s0 = cnt["pe"]

        @block.scalar
        def _(s):
            s.wait_ge(sems["ld"], ld_small)
            s.wait_ge(sems["pe"], pe_s0)
            for m in range(NKD):
                for n in range(2):
                    inc(s.activation(out=s_h[:, m * T + n * TH:m * T + n * TH + TH],
                                     in_=P[m * 2 + n][:], func=AF.Identity,
                                     bias=s_bin[:, m:m + 1], scale=1.0), "sc")
    sc_h0 = cnt["sc"]
    barrier("s0")

    e2 = ExitStack()
    s_sa = mk(e2, "s_sa", [128, NKD * T], F32)
    s_s2 = mk(e2, "s_s2", [128, NKD * T], F32)
    s_y = mk(e2, "s_y", [128, NKD * T], F32)
    s_scr = mk(e2, "s_scr", [128, 8192], F32)
    s_wab = mk(e2, "s_wab", [128, 2048], F32)
    s_wa = s_wab[:, 0:1024]
    s_wb = s_wab[:, 1024:2048]
    s_wf = mk(e2, "s_wf", [128, 8192], F32)
    s_h2f = s_scr[:, 0:4096]
    s_ybf = mk(e2, "s_ybf", [128, NKD * T], BF16)
    s_eg = mk(e2, "s_eg", [128, 32], F32)
    s_ep = mk(e2, "s_ep", [128, 8], F32)
    s_cand = mk(e2, "s_cand", [128, 64], F32)

    class _DrainV:
        def __init__(self, v):
            self._v = v

        def __getattr__(self, name):
            f = getattr(self._v, name)
            if name in ("drain", "wait_ge"):
                return f

            def wrap(*a, **k):
                r = f(*a, **k)
                self._v.drain()
                return r
            return wrap

    def tcol(dt_, c):
        return s_tiny[:, dt_ * 56 + c:dt_ * 56 + c + 1]

    def cgt(dt_):
        return s_scr[:, dt_ * 2048:dt_ * 2048 + 1024]

    def sgt(dt_):
        return s_scr[:, dt_ * 2048 + 1024:dt_ * 2048 + 2048]

    def rt(dt_):
        return s_y[:, dt_ * 1024:(dt_ + 1) * 1024]

    def scal_chain(lyr):
        @block.vector
        def _(v):
            v.wait_ge(sems["ld"], ld_small)
            for dt_ in range(NKD):
                v.tensor_scalar(out=tcol(dt_, 51), in0=par("pre", dt_, lyr),
                                scalar1=par("pre", dt_, lyr), scalar2=None, op0=ALU.mult)
            v.drain()
            for dt_ in range(NKD):
                ins = v.scalar_tensor_tensor(out=tcol(dt_, 51), in0=par("pim", dt_, lyr),
                                             scalar=par("pim", dt_, lyr), in1=tcol(dt_, 51),
                                             op0=ALU.mult, op1=ALU.add)
            inc(ins, "ve")
        c1 = cnt["ve"]

        @block.scalar
        def _(s):
            s.wait_ge(sems["ve"], c1)
            for dt_ in range(NKD):
                s.activation(out=tcol(dt_, 51), in_=tcol(dt_, 51), func=AF.Sqrt,
                             bias=s_cst[:, 1:2], scale=1.0)
            s.drain()
            for dt_ in range(NKD):
                ins = s.activation(out=tcol(dt_, 43), in_=tcol(dt_, 51), func=AF.Tanh)
            inc(ins, "sc")
        c2 = cnt["sc"]

        @block.vector
        def _(v):
            v.wait_ge(sems["sc"], c2)
            for dt_ in range(NKD):
                v.reciprocal(out=tcol(dt_, 52), in_=tcol(dt_, 51))
            v.drain()
            for dt_ in range(NKD):
                v.tensor_scalar(out=tcol(dt_, 47), in0=par("pre", dt_, lyr),
                                scalar1=tcol(dt_, 52), scalar2=None, op0=ALU.mult)
                v.tensor_scalar(out=tcol(dt_, 48), in0=par("pim", dt_, lyr),
                                scalar1=tcol(dt_, 52), scalar2=None, op0=ALU.mult)
            v.drain()
            for dt_ in range(NKD):
                v.tensor_copy(out=tcol(dt_, 0), in_=tcol(dt_, 47))
                v.tensor_copy(out=tcol(dt_, 11), in_=tcol(dt_, 48))
                v.tensor_scalar(out=tcol(dt_, 22), in0=tcol(dt_, 48), scalar1=-1.0,
                                scalar2=None, op0=ALU.mult)
                v.tensor_copy(out=tcol(dt_, 32), in_=tcol(dt_, 43))
            v.drain()
            for k in range(1, 11):
                for dt_ in range(NKD):
                    v.tensor_scalar(out=tcol(dt_, 53), in0=tcol(dt_, k - 1),
                                    scalar1=tcol(dt_, k - 1), scalar2=None, op0=ALU.mult)
                v.drain()
                for dt_ in range(NKD):
                    v.scalar_tensor_tensor(out=tcol(dt_, k), in0=tcol(dt_, 11 + k - 1),
                                           scalar=tcol(dt_, 22 + k - 1), in1=tcol(dt_, 53),
                                           op0=ALU.mult, op1=ALU.add)
                    v.tensor_scalar(out=tcol(dt_, 11 + k), in0=tcol(dt_, 11 + k - 1),
                                    scalar1=tcol(dt_, k - 1), scalar2=2.0,
                                    op0=ALU.mult, op1=ALU.mult)
                    v.tensor_scalar(out=tcol(dt_, 32 + k), in0=tcol(dt_, 32 + k - 1),
                                    scalar1=tcol(dt_, 32 + k - 1), scalar2=None, op0=ALU.mult)
                v.drain()
                if k <= 9:
                    for dt_ in range(NKD):
                        v.tensor_scalar(out=tcol(dt_, 22 + k), in0=tcol(dt_, 11 + k),
                                        scalar1=-1.0, scalar2=None, op0=ALU.mult)
                    v.drain()
            for dt_ in range(NKD):
                q = dt_ * 16
                v.memset(s_cand[:, q:q + 1], 1.0)
                v.memset(s_cand[:, q + 4:q + 5], 0.0)
                v.tensor_copy(out=s_cand[:, q + 1:q + 2], in_=tcol(dt_, 10))
                v.tensor_copy(out=s_cand[:, q + 5:q + 6], in_=tcol(dt_, 21))
                v.memset(s_cand[:, q + 8:q + 9], 1.0)
                v.tensor_copy(out=s_cand[:, q + 9:q + 10], in_=tcol(dt_, 42))
                v.tensor_scalar(out=s_cand[:, q + 10:q + 11], in0=tcol(dt_, 42),
                                scalar1=tcol(dt_, 42), scalar2=None, op0=ALU.mult)
            v.drain()
            for j in (2, 3):
                for dt_ in range(NKD):
                    q = dt_ * 16
                    v.tensor_scalar(out=tcol(dt_, 53), in0=s_cand[:, q + j - 1:q + j],
                                    scalar1=tcol(dt_, 10), scalar2=None, op0=ALU.mult)
                    v.tensor_scalar(out=tcol(dt_, 54), in0=s_cand[:, q + 3 + j:q + 4 + j],
                                    scalar1=tcol(dt_, 21), scalar2=None, op0=ALU.mult)
                v.drain()
                for dt_ in range(NKD):
                    q = dt_ * 16
                    v.tensor_tensor(out=s_cand[:, q + j:q + j + 1], in0=tcol(dt_, 53),
                                    in1=tcol(dt_, 54), op=ALU.subtract)
                    v.tensor_scalar(out=tcol(dt_, 53), in0=s_cand[:, q + 3 + j:q + 4 + j],
                                    scalar1=tcol(dt_, 10), scalar2=None, op0=ALU.mult)
                    v.tensor_scalar(out=tcol(dt_, 54), in0=s_cand[:, q + j - 1:q + j],
                                    scalar1=tcol(dt_, 21), scalar2=None, op0=ALU.mult)
                v.drain()
                for dt_ in range(NKD):
                    q = dt_ * 16
                    v.tensor_tensor(out=s_cand[:, q + 4 + j:q + 5 + j], in0=tcol(dt_, 53),
                                    in1=tcol(dt_, 54), op=ALU.add)
                    if j == 3:
                        v.tensor_scalar(out=s_cand[:, q + 11:q + 12],
                                        in0=s_cand[:, q + 10:q + 11],
                                        scalar1=tcol(dt_, 42), scalar2=None, op0=ALU.mult)
                v.drain()
            for dt_ in range(NKD):
                q = dt_ * 16
                v.tensor_tensor(out=s_cand[:, q:q + 4], in0=s_cand[:, q:q + 4],
                                in1=s_selv[:], op=ALU.mult)
                v.tensor_tensor(out=s_cand[:, q + 4:q + 8], in0=s_cand[:, q + 4:q + 8],
                                in1=s_selv[:], op=ALU.mult)
                v.tensor_tensor(out=s_cand[:, q + 8:q + 12], in0=s_cand[:, q + 8:q + 12],
                                in1=s_selv[:], op=ALU.mult)
            v.drain()
            for dt_ in range(NKD):
                q = dt_ * 16
                v.tensor_reduce(out=tcol(dt_, 44), in_=s_cand[:, q:q + 4], axis=AXF, op=ALU.add)
                v.tensor_reduce(out=tcol(dt_, 45), in_=s_cand[:, q + 4:q + 8], axis=AXF, op=ALU.add)
                v.tensor_reduce(out=tcol(dt_, 46), in_=s_cand[:, q + 8:q + 12], axis=AXF, op=ALU.add)
            v.drain()
            for dt_ in range(NKD):
                v.tensor_scalar(out=tcol(dt_, 53), in0=par("lre", dt_, lyr),
                                scalar1=tcol(dt_, 47), scalar2=None, op0=ALU.mult)
                v.tensor_scalar(out=tcol(dt_, 54), in0=par("lim", dt_, lyr),
                                scalar1=tcol(dt_, 48), scalar2=None, op0=ALU.mult)
                v.tensor_scalar(out=tcol(dt_, 55), in0=par("lre", dt_, lyr),
                                scalar1=tcol(dt_, 48), scalar2=None, op0=ALU.mult)
                v.tensor_scalar(out=tcol(dt_, 51), in0=par("lim", dt_, lyr),
                                scalar1=tcol(dt_, 47), scalar2=None, op0=ALU.mult)
            v.drain()
            for dt_ in range(NKD):
                v.tensor_tensor(out=tcol(dt_, 49), in0=tcol(dt_, 53), in1=tcol(dt_, 54),
                                op=ALU.subtract)
                ins = v.tensor_tensor(out=tcol(dt_, 50), in0=tcol(dt_, 55), in1=tcol(dt_, 51),
                                      op=ALU.add)
            v.drain()
            inc(ins, "ve")

    def tab_pass(want_r, pre_wait=None):
        @block.vector
        def _(v):
            if pre_wait is not None:
                v.wait_ge(sems[pre_wait[0]], pre_wait[1])
            for dt_ in range(NKD):
                v.tensor_copy(out=cgt(dt_)[:, 0:1], in_=tcol(dt_, 44))
                v.tensor_copy(out=sgt(dt_)[:, 0:1], in_=tcol(dt_, 45))
                if want_r:
                    v.memset(rt(dt_)[:, 0:1], 1.0)
            v.drain()
            for k in range(10):
                w = 1 << k
                for dt_ in range(NKD):
                    v.tensor_scalar(out=s_wab[:, dt_ * 512:dt_ * 512 + w], in0=cgt(dt_)[:, 0:w],
                                    scalar1=tcol(dt_, k), scalar2=None, op0=ALU.mult)
                v.drain()
                for dt_ in range(NKD):
                    v.scalar_tensor_tensor(out=cgt(dt_)[:, w:2 * w], in0=sgt(dt_)[:, 0:w],
                                           scalar=tcol(dt_, 22 + k),
                                           in1=s_wab[:, dt_ * 512:dt_ * 512 + w],
                                           op0=ALU.mult, op1=ALU.add)
                v.drain()
                for dt_ in range(NKD):
                    v.tensor_scalar(out=s_wab[:, dt_ * 512:dt_ * 512 + w], in0=cgt(dt_)[:, 0:w],
                                    scalar1=tcol(dt_, 11 + k), scalar2=None, op0=ALU.mult)
                v.drain()
                for dt_ in range(NKD):
                    v.scalar_tensor_tensor(out=sgt(dt_)[:, w:2 * w], in0=sgt(dt_)[:, 0:w],
                                           scalar=tcol(dt_, k),
                                           in1=s_wab[:, dt_ * 512:dt_ * 512 + w],
                                           op0=ALU.mult, op1=ALU.add)
                    if want_r:
                        v.tensor_scalar(out=rt(dt_)[:, w:2 * w], in0=rt(dt_)[:, 0:w],
                                        scalar1=tcol(dt_, 32 + k), scalar2=None, op0=ALU.mult)
                ins = v.drain()
            inc(v.engine_nop(), "ve")

    def ln_norm_affine(dst, func, wname, bname, lyr, ve_rows, w_is_lnl=False,
                       pre_waits=()):
        for n in range(2):
            extra = list(pre_waits) if n == 0 else [("sc", cnt["sc"])]
            pe_bc = ln_bcast(n, ve_rows, extra_waits=extra)

            @block.vector
            def _(v, n=n, pe_bc=pe_bc):
                v.wait_ge(sems["pe"], pe_bc)
                for dt_ in range(NKD):
                    sl = slice(dt_ * T + n * TH, dt_ * T + n * TH + TH)
                    v.tensor_tensor(out=s_y[:, sl], in0=s_h[:, sl], in1=P[3][:],
                                    op=ALU.subtract)
                v.drain()
                for dt_ in range(NKD):
                    sl = slice(dt_ * T + n * TH, dt_ * T + n * TH + TH)
                    ins = v.tensor_tensor(out=s_y[:, sl], in0=s_y[:, sl], in1=P[4][:],
                                          op=ALU.mult)
                inc(ins, "ve")
            ve_n = cnt["ve"]

            @block.scalar
            def _(s, n=n, ve_n=ve_n):
                s.wait_ge(sems["ve"], ve_n)
                for dt_ in range(NKD):
                    sl = slice(dt_ * T + n * TH, dt_ * T + n * TH + TH)
                    if w_is_lnl:
                        sc_, bi_ = s_lnl[:, dt_:dt_ + 1], s_lnl[:, NKD + dt_:NKD + dt_ + 1]
                    else:
                        sc_, bi_ = par(wname, dt_, lyr), par(bname, dt_, lyr)
                    ins = s.activation(out=dst[:, sl], in_=s_y[:, sl], func=func,
                                       scale=sc_, bias=bi_)
                inc(ins, "sc")
        return cnt["sc"]

    if dbg:
        @block.sync
        def _(sy):
            sy.wait_ge(sems["sc"], sc_h0)
            dmai(sy, dbg_out["dbg_h0"], s_h[:], "st")


        @block.vector
        def _(v):
            v.wait_ge(sems["st"], cnt["st"])
    @block.gpsimd
    def _(g):
        dmai(g, s_wsh[:], dr["woutT"], "lg")
    lg_wout = cnt["lg"]

    for lyr in range(DEPTH):
        @block.scalar
        def _(s, lyr=lyr):
            s.wait_ge(sems["ve"], cnt["ve"])
            if lyr == 0:
                s.wait_ge(sems["sc"], sc_h0)
            inc(s.activation(out=s_h2f[:], in_=s_h[:], func=AF.Square), "sc")
        ve_rows = ln_stats_rows(s_h, s_h2f, NKD, DIM, [("sc", cnt["sc"])])
        sc_y = ln_norm_affine(s_y, AF.Silu, "ln1w", "ln1b", lyr, ve_rows)

        if dbg and lyr == 0:
            @block.sync
            def _(sy):
                sy.wait_ge(sems["sc"], sc_y)
                dmai(sy, dbg_out["dbg_ya"], s_y[:], "st")


            @block.vector
            def _(v):
                v.wait_ge(sems["st"], cnt["st"])
        scal_chain(lyr)
        tab_pass(False, pre_wait=("sc", sc_y))
        for dt_ in range(NKD):
            @block.vector
            def _(v, dt_=dt_, lyr=lyr):
                v.tensor_tensor(out=s_wa[:], in0=cgt(dt_), in1=s_y[:, dt_ * T:(dt_ + 1) * T],
                                op=ALU.mult)
                v.tensor_tensor(out=s_wb[:], in0=sgt(dt_), in1=s_y[:, dt_ * T:(dt_ + 1) * T],
                                op=ALU.mult)
                v.drain()
                dec = tcol(dt_, 43).broadcast_to((128, TH))
                for (src, dst2) in [(s_wa, s_sa), (s_wb, s_s2)]:
                    v.tensor_tensor_scan(out=dst2[:, dt_ * T:dt_ * T + TH], data0=dec,
                                         data1=src[:, 0:TH], initial=0.0,
                                         op0=ALU.mult, op1=ALU.add)
                    v.drain()
                    ins = v.tensor_tensor_scan(out=dst2[:, dt_ * T + TH:dt_ * T + T], data0=dec,
                                               data1=src[:, TH:T],
                                               initial=dst2[:, dt_ * T + TH - 1:dt_ * T + TH],
                                               op0=ALU.mult, op1=ALU.add)
                    v.drain()
                v.tensor_copy(out=s_ep[:, dt_:dt_ + 1], in_=s_sa[:, dt_ * T + T - 1:dt_ * T + T])
                ins = v.tensor_copy(out=s_ep[:, 4 + dt_:5 + dt_],
                                    in_=s_s2[:, dt_ * T + T - 1:dt_ * T + T])
                inc(ins, "ve")
            ve_scan = cnt["ve"]
        @block.sync
        def _(sy, lyr=lyr, ve_scan=ve_scan):
            sy.wait_ge(sems["ve"], ve_scan)
            if lyr > 0:
                sy.wait_ge(sems["cc"], lyr)
            dmai(sy, cc_in.ap(), s_ep[:], slow=True)
        ld_e = cnt["ld"]

        @block.gpsimd
        def _(g, ld_e=ld_e):
            g.wait_ge(sems["ld"], ld_e)
            inc(g.collective_compute(
                "AllGather", ALU.bypass,
                replica_groups=[[0, 1, 2, 3], [4, 5, 6, 7]],
                ins=[cc_in.ap().opt()], outs=[cc_out.ap().opt()]), "cc")
        cc_n = cnt["cc"]

        @block.sync
        def _(sy, cc_n=cc_n):
            sy.wait_ge(sems["cc"], cc_n)
            dmai(sy, s_eg[:].rearrange("p (j c) -> p j c", c=8),
                 cc_out.ap().rearrange("(j p) c -> p j c", p=128))
        ld_eg = cnt["ld"]

        tab_pass(True, pre_wait=("ld", ld_eg))

        @block.vector
        def _(v, lyr=lyr):
            v = _DrainV(v)
            for dt_ in range(NKD):
                v.tensor_scalar(out=tcol(dt_, 53), in0=tcol(dt_, 42), scalar1=tcol(dt_, 42),
                                scalar2=None, op0=ALU.mult)
            v.drain()
            for half in range(2):
                for dt_ in range(NKD):
                    cofs = half * 4 + dt_
                    E = [s_eg[:, j * 8 + cofs:j * 8 + cofs + 1] for j in range(4)]
                    q = dt_ * 16
                    cd = s_cand[:, q:q + 4]
                    v.memset(cd[:, 0:1], 0.0)
                    v.tensor_copy(out=cd[:, 1:2], in_=E[0])
                    v.scalar_tensor_tensor(out=cd[:, 2:3], in0=E[0], scalar=tcol(dt_, 42),
                                           in1=E[1], op0=ALU.mult, op1=ALU.add)
                    v.scalar_tensor_tensor(out=cd[:, 3:4], in0=E[1], scalar=tcol(dt_, 42),
                                           in1=E[2], op0=ALU.mult, op1=ALU.add)
                v.drain()
                for dt_ in range(NKD):
                    q = dt_ * 16
                    E0 = s_eg[:, 0 * 8 + half * 4 + dt_:0 * 8 + half * 4 + dt_ + 1]
                    v.scalar_tensor_tensor(out=s_cand[:, q + 3:q + 4], in0=E0,
                                           scalar=tcol(dt_, 53), in1=s_cand[:, q + 3:q + 4],
                                           op0=ALU.mult, op1=ALU.add)
                v.drain()
                for dt_ in range(NKD):
                    q = dt_ * 16
                    v.tensor_tensor(out=s_cand[:, q:q + 4], in0=s_cand[:, q:q + 4],
                                    in1=s_selv[:], op=ALU.mult)
                v.drain()
                for dt_ in range(NKD):
                    q = dt_ * 16
                    v.tensor_reduce(out=tcol(dt_, 51 + half), in_=s_cand[:, q:q + 4],
                                    axis=AXF, op=ALU.add)
                v.drain()
            for dt_ in range(NKD):
                v.tensor_scalar(out=tcol(dt_, 54), in0=tcol(dt_, 49), scalar1=tcol(dt_, 46),
                                scalar2=None, op0=ALU.mult)
                v.tensor_scalar(out=tcol(dt_, 55), in0=tcol(dt_, 50), scalar1=tcol(dt_, 46),
                                scalar2=None, op0=ALU.mult)
            v.drain()
            for dt_ in range(NKD):
                v.tensor_tensor(out=tcol(dt_, 54), in0=tcol(dt_, 51), in1=tcol(dt_, 54),
                                op=ALU.add)
                v.tensor_tensor(out=tcol(dt_, 55), in0=tcol(dt_, 52), in1=tcol(dt_, 55),
                                op=ALU.subtract)
            v.drain()
            for dt_ in range(NKD):
                v.tensor_scalar(out=tcol(dt_, 54), in0=tcol(dt_, 54), scalar1=tcol(dt_, 43),
                                scalar2=None, op0=ALU.mult)
                ins = v.tensor_scalar(out=tcol(dt_, 55), in0=tcol(dt_, 55), scalar1=tcol(dt_, 43),
                                      scalar2=None, op0=ALU.mult)
            inc(ins, "ve")

        for dt_ in range(NKD):
            @block.vector
            def _(v, dt_=dt_, lyr=lyr):
                v.scalar_tensor_tensor(out=s_sa[:, dt_ * T:(dt_ + 1) * T], in0=rt(dt_),
                                       scalar=tcol(dt_, 54),
                                       in1=s_sa[:, dt_ * T:(dt_ + 1) * T],
                                       op0=ALU.mult, op1=ALU.add)
                v.scalar_tensor_tensor(out=s_s2[:, dt_ * T:(dt_ + 1) * T], in0=rt(dt_),
                                       scalar=tcol(dt_, 55),
                                       in1=s_s2[:, dt_ * T:(dt_ + 1) * T],
                                       op0=ALU.mult, op1=ALU.add)
                v.drain()
                v.tensor_tensor(out=s_wa[:], in0=cgt(dt_), in1=s_sa[:, dt_ * T:(dt_ + 1) * T],
                                op=ALU.mult)
                v.tensor_tensor(out=s_wb[:], in0=sgt(dt_), in1=s_s2[:, dt_ * T:(dt_ + 1) * T],
                                op=ALU.mult)
                v.drain()
                v.tensor_tensor(out=s_wa[:], in0=s_wa[:], in1=s_wb[:], op=ALU.add)
                v.drain()
                ins = v.scalar_tensor_tensor(out=s_h[:, dt_ * T:(dt_ + 1) * T], in0=s_wa[:],
                                             scalar=par("scl", dt_, lyr),
                                             in1=s_h[:, dt_ * T:(dt_ + 1) * T],
                                             op0=ALU.mult, op1=ALU.add)
                inc(ins, "ve")
        ve_hmid = cnt["ve"]

        if dbg and lyr == 0:
            @block.sync
            def _(sy):
                sy.wait_ge(sems["ve"], ve_hmid)
                dmai(sy, dbg_out["dbg_hd0"], s_h[:], "st")


            @block.vector
            def _(v):
                v.wait_ge(sems["st"], cnt["st"])
        @block.scalar
        def _(s, ve_hmid=ve_hmid):
            s.wait_ge(sems["ve"], ve_hmid)
            inc(s.activation(out=s_h2f[:], in_=s_h[:], func=AF.Square), "sc")
        ve_rows2 = ln_stats_rows(s_h, s_h2f, NKD, DIM, [("sc", cnt["sc"])])
        sc_ybf = ln_norm_affine(s_y, AF.Identity, "ln2w", "ln2b", lyr, ve_rows2)

        for n in range(2):
            pe_prev = cnt["pe"]

            @block.sync
            def _(sy, lyr=lyr, n=n, pe_prev=pe_prev):
                if lyr == 0 and n == 0:
                    sy.wait_ge(sems["pe"], pe_s0)
                else:
                    sy.wait_ge(sems["pe"], pe_prev)
                for q in range(2):
                    dmai(sy, s_wf[:, q * 4096:(q + 1) * 4096],
                         dr["w1T"][lyr][:, q * 4096:(q + 1) * 4096])
            ld_w1 = cnt["ld"]
            pe_w1_0 = cnt["pe"]
            sc_y2_0 = cnt["sc"]

            @block.tensor
            def _(t, n=n, lyr=lyr, pe_w1_0=pe_w1_0, sc_y2_0=sc_y2_0, ld_w1=ld_w1):
                t.wait_ge(sems["sc"], sc_ybf)
                t.wait_ge(sems["ld"], ld_w1)
                for m in range(NKF):
                    if m >= 2:
                        t.wait_ge(sems["sc"], sc_y2_0 + m - 1)
                    else:
                        t.wait_ge(sems["sc"], sc_y2_0)
                    for k in range(NKD):
                        ins = t.matmul(P[m % 2][:],
                                       s_wf[:, k * DFF + m * 128:k * DFF + m * 128 + 128],
                                       s_y[:, k * T + n * TH:k * T + n * TH + TH],
                                       start=(k == 0), stop=(k == NKD - 1))
                    inc(ins, "pe")
            pe_w1e = cnt["pe"]

            @block.scalar
            def _(s, n=n, lyr=lyr, pe_w1_0=pe_w1_0):
                for m in range(NKF):
                    s.wait_ge(sems["pe"], pe_w1_0 + m + 1)
                    bc = (m * DEPTH) + lyr
                    inc(s.activation(out=s_scr[:, m * TH:(m + 1) * TH], in_=P[m % 2][:],
                                     func=AF.Silu, bias=s_b1[:, bc:bc + 1], scale=1.0), "sc")
            sc_y2 = cnt["sc"]

            @block.sync
            def _(sy, lyr=lyr, pe_w1e=pe_w1e):
                sy.wait_ge(sems["pe"], pe_w1e)
                for q in range(2):
                    dmai(sy, s_wf[:, q * 4096:(q + 1) * 4096],
                         dr["w2T"][lyr][:, q * 4096:(q + 1) * 4096])
            ld_w2 = cnt["ld"]
            pe_w2_0 = cnt["pe"]
            ve_h_0 = cnt["ve"]

            @block.tensor
            def _(t, n=n, sc_y2=sc_y2, pe_w2_0=pe_w2_0, ve_h_0=ve_h_0, ld_w2=ld_w2):
                t.wait_ge(sems["sc"], sc_y2)
                t.wait_ge(sems["ld"], ld_w2)
                for m in range(NKD):
                    if m >= 2:
                        t.wait_ge(sems["ve"], ve_h_0 + m - 1)
                    else:
                        t.wait_ge(sems["ve"], ve_h_0)
                    for k in range(NKF):
                        ins = t.matmul(P[4 + m % 2][:],
                                       s_wf[:, k * DIM + m * 128:k * DIM + m * 128 + 128],
                                       s_scr[:, k * TH:(k + 1) * TH],
                                       start=(k == 0), stop=(k == NKF - 1))
                    inc(ins, "pe")

            @block.vector
            def _(v, n=n, lyr=lyr, pe_w2_0=pe_w2_0):
                for m in range(NKD):
                    v.wait_ge(sems["pe"], pe_w2_0 + m + 1)
                    bc = (m * DEPTH) + lyr
                    inc(v.scalar_tensor_tensor(
                        out=s_h[:, m * T + n * TH:m * T + n * TH + TH],
                        in0=P[4 + m % 2][:], scalar=s_b2[:, bc:bc + 1],
                        in1=s_h[:, m * T + n * TH:m * T + n * TH + TH],
                        op0=ALU.add, op1=ALU.add), "ve")
        pe_ffn = cnt["pe"]
        barrier(f"d{lyr}")

    if dbg:
        @block.sync
        def _(sy):
            sy.wait_ge(sems["ve"], cnt["ve"])
            dmai(sy, dbg_out["dbg_h"], s_h[:], "st")


        @block.vector
        def _(v):
            v.wait_ge(sems["st"], cnt["st"])
    @block.scalar
    def _(s):
        s.wait_ge(sems["ve"], cnt["ve"])
        inc(s.activation(out=s_h2f[:], in_=s_h[:], func=AF.Square), "sc")
    ve_rows3 = ln_stats_rows(s_h, s_h2f, NKD, DIM, [("sc", cnt["sc"])])
    sc_lnl = ln_norm_affine(s_ybf, AF.Identity, None, None, 0, ve_rows3, w_is_lnl=True)
    barrier("lnl")
    if dbg:
        @block.sync
        def _(sy):
            sy.wait_ge(sems["sc"], sc_lnl)
            dmai(sy, dbg_out["dbg_y"], s_y[:], "st")


        @block.vector
        def _(v):
            v.wait_ge(sems["st"], cnt["st"])
    with ExitStack() as e3:
        s_xf32 = [s_scr[:, 0:1024], s_scr[:, 1024:2048]]
        s_xr = s_scr[:, 2048:3072]
        s_xrb = [mk(e3, f"s_xrb_{i}", [128, T], BF16) for i in range(2)]
        s_x2b = mk(e3, "s_x2b", [128, T], BF16)

        ld_x0 = cnt["ld"]
        pe_wo_0 = cnt["pe"]
        ve_xr_0 = cnt["ve"]
        sc_xr_0 = cnt["sc"]
        st_xr_0 = cnt["st"]
        pe_p3 = cnt["pe"]
        for m in range(NKV):
            @block.sync
            def _(sy, m=m):
                if m == 0:
                    sy.wait_ge(sems["pe"], pe_p3)
                if m >= 2:
                    sy.wait_ge(sems["ve"], ve_xr_0 + (m - 2) * 2 + 2)
                dmai(sy, s_xf32[m % 2][:], dr["xT"][m * 128:(m + 1) * 128, :])
            for n in range(2):
                i = m * 2 + n

                @block.tensor
                def _(t, m=m, n=n, i=i):
                    if i == 0:
                        t.wait_ge(sems["sc"], sc_lnl)
                        t.wait_ge(sems["lg"], lg_wout)
                    if i >= 2:
                        t.wait_ge(sems["ve"], ve_xr_0 + i - 1)
                    for k in range(NKD):
                        ins = t.matmul(P[i % 2][:],
                                       s_wsh[:, k * V + m * 128:k * V + m * 128 + 128],
                                       s_ybf[:, k * T + n * TH:k * T + n * TH + TH],
                                       start=(k == 0), stop=(k == NKD - 1))
                    inc(ins, "pe")

                @block.vector
                def _(v, m=m, n=n, i=i):
                    v.wait_ge(sems["ld"], ld_x0 + (m + 1) * 16)
                    v.wait_ge(sems["pe"], pe_wo_0 + 2 * i + 1)
                    if i >= 2:
                        v.wait_ge(sems["sc"], sc_xr_0 + (i - 2) * 2 + 2)
                    inc(v.scalar_tensor_tensor(
                        out=s_xr[:, n * TH:n * TH + TH], in0=P[i % 2][:],
                        scalar=s_bout[:, m:m + 1],
                        in1=s_xf32[m % 2][:, n * TH:n * TH + TH],
                        op0=ALU.add, op1=ALU.add), "ve")

                @block.scalar
                def _(s, m=m, n=n, i=i):
                    s.wait_ge(sems["ve"], ve_xr_0 + i + 1)
                    if m >= 2 and n == 0:
                        s.wait_ge(sems["st"], st_xr_0 + (m - 1) * 16)
                    inc(s.activation(out=s_xrb[m % 2][:, n * TH:n * TH + TH],
                                     in_=s_xr[:, n * TH:n * TH + TH], func=AF.Copy), "sc")
                    inc(s.activation(out=s_x2b[:, n * TH:n * TH + TH],
                                     in_=s_xr[:, n * TH:n * TH + TH], func=AF.Square), "sc")

                @block.tensor
                def _(t, m=m, n=n, i=i):
                    t.wait_ge(sems["sc"], sc_xr_0 + i * 2 + 2)
                    pr = P[2] if n == 0 else P[5]
                    t.matmul(pr[0:1, :], s_onc[:],
                             s_xrb[m % 2][:, n * TH:n * TH + TH],
                             start=(m == 0), stop=(m == NKV - 1))
                    inc(t.matmul(pr[32:33, :], s_onc[:],
                                 s_x2b[:, n * TH:n * TH + TH],
                                 start=(m == 0), stop=(m == NKV - 1)), "pe")

            @block.sync
            def _(sy, m=m):
                sy.wait_ge(sems["sc"], sc_xr_0 + m * 4 + 4)
                dmai(sy, xrd.ap()[:, m:m + 1, :].rearrange("p a b -> p (a b)"),
                     s_xrb[m % 2][:], "st")

        pe_allst = cnt["pe"]

        @block.vector
        def _(v):
            v.wait_ge(sems["pe"], pe_allst)
            for n in range(2):
                pr = P[2] if n == 0 else P[5]
                v.tensor_scalar(out=s_row[0:1, n * TH:n * TH + TH],
                                in0=pr[0:1, :], scalar1=1.0 / V,
                                scalar2=None, op0=ALU.mult)
                v.tensor_scalar(out=s_row[0:1, 1024 + n * TH:1024 + n * TH + TH],
                                in0=pr[32:33, :], scalar1=1.0 / V,
                                scalar2=None, op0=ALU.mult)
            v.drain()
            for n in range(2):
                v.tensor_tensor(out=s_row[0:1, 2048:2048 + TH],
                                in0=s_row[0:1, n * TH:n * TH + TH],
                                in1=s_row[0:1, n * TH:n * TH + TH], op=ALU.mult)
                v.drain()
                ins = v.tensor_tensor(out=s_row[0:1, 1024 + n * TH:1024 + n * TH + TH],
                                      in0=s_row[0:1, 1024 + n * TH:1024 + n * TH + TH],
                                      in1=s_row[0:1, 2048:2048 + TH],
                                      op=ALU.subtract)
                v.drain()
            inc(ins, "ve")
        ve_v4 = cnt["ve"]

        @block.scalar
        def _(s):
            s.wait_ge(sems["ve"], ve_v4)
            for n in range(2):
                ins = s.activation(out=s_row[0:1, 1024 + n * TH:1024 + n * TH + TH],
                                   in_=s_row[0:1, 1024 + n * TH:1024 + n * TH + TH],
                                   func=AF.Sqrt, bias=s_cst[0:1, 0:1], scale=1.0)
            inc(ins, "sc")
        sc_q4 = cnt["sc"]

        @block.vector
        def _(v):
            v.wait_ge(sems["sc"], sc_q4)
            for n in range(2):
                ins = v.reciprocal(out=s_row[0:1, 1024 + n * TH:1024 + n * TH + TH],
                                   in_=s_row[0:1, 1024 + n * TH:1024 + n * TH + TH])
            inc(ins, "ve")
        ve_rows4 = cnt["ve"]

    e2.close()
    e_w.close()

    e4 = ExitStack()
    s_x1 = mk(e4, "s_x1", [128, NKV * T], BF16)
    s_xg = mk(e4, "s_xg", [32, T], BF16)
    s_xrr = [mk(e4, f"s_xrr_{i}", [128, TH], BF16) for i in range(2)]
    s_qw = [mk(e4, f"s_qw_{i}", [128, NKV * 128], BF16) for i in range(4)]
    s_gb = [mk(e4, f"s_gb_{i}", [128, V], F32) for i in range(4)]
    s_nzt = [mk(e4, f"s_nzt_{i}", [32, 256], BF16) for i in range(2)]
    s_hst = [mk(e4, f"s_hst_{i}", [128, TH], BF16) for i in range(4)]
    s_gt = [mk(e4, f"s_gt_{i}", [128, TH], F32) for i in range(2)]

    for n in range(2):
        pe_bc4 = ln_bcast(n, ve_rows4,
                          extra_waits=([] if n == 0 else [("ve", cnt["ve"])]))
        ld_xr_0 = cnt["ld"]
        ve_x1_0 = cnt["ve"]

        @block.sync
        def _(sy, n=n, ve_x1_0=ve_x1_0):
            sy.wait_ge(sems["st"], st_xr_0 + NKV * 16)
            for m in range(NKV):
                if m >= 2:
                    sy.wait_ge(sems["ve"], ve_x1_0 + (m - 2) * 2 + 2)
                else:
                    sy.wait_ge(sems["ve"], ve_x1_0)
                dmai(sy, s_xrr[m % 2][:],
                     xrd.ap()[:, m:m + 1, n * TH:n * TH + TH].rearrange("p a b -> p (a b)"))

        @block.vector
        def _(v, n=n, pe_bc4=pe_bc4, ld_xr_0=ld_xr_0):
            v.wait_ge(sems["pe"], pe_bc4)
            for m in range(NKV):
                v.wait_ge(sems["ld"], ld_xr_0 + (m + 1) * 16)
                sl = slice(m * T + n * TH, m * T + n * TH + TH)
                v.tensor_tensor(out=s_x1[:, sl], in0=s_xrr[m % 2][:], in1=P[3][:],
                                op=ALU.subtract)
                v.drain()
                inc(v.tensor_tensor(out=s_x1[:, sl], in0=s_x1[:, sl], in1=P[4][:],
                                    op=ALU.mult), "ve", 2)
    ve_x1 = cnt["ve"]

    @block.sync
    def _(sy):
        sy.wait_ge(sems["ve"], ve_x1)
        dmai(sy, x1d.ap().rearrange("p a b -> p (a b)"), s_x1[:], "st")
    st_x1 = cnt["st"]
    barrier("x1")

    if dbg:
        @block.scalar
        def _(s):
            s.wait_ge(sems["ve"], ve_x1)
            inc(s.activation(out=s_sa[:, 0:NKD * T] if False else s_h[:], in_=s_x1[:, 0:NKD * T],
                             func=AF.Copy), "sc")

        @block.sync
        def _(sy):
            sy.wait_ge(sems["sc"], cnt["sc"])
            dmai(sy, dbg_out["dbg_x1"][:, 0:NKD * T], s_h[:], "st")


        @block.vector
        def _(v):
            v.wait_ge(sems["st"], cnt["st"])
    sc_xg_0 = cnt["sc"]
    for n in range(2):
        @block.tensor
        def _(t, n=n):
            t.wait_ge(sems["ve"], ve_x1)
            if n == 1:
                t.wait_ge(sems["sc"], sc_xg_0 + 1)
            for k in range(NKV):
                ins = t.matmul(P[6][0:32, :], s_e32[:, k * 32:(k + 1) * 32],
                               s_x1[:, k * T + n * TH:k * T + n * TH + TH],
                               start=(k == 0), stop=(k == NKV - 1))
            inc(ins, "pe")
        pe_xg = cnt["pe"]

        @block.scalar
        def _(s, n=n, pe_xg=pe_xg):
            s.wait_ge(sems["pe"], pe_xg)
            inc(s.activation(out=s_xg[0:32, n * TH:n * TH + TH], in_=P[6][0:32, :],
                             func=AF.Copy), "sc")
    sc_xgv = cnt["sc"]
    barrier("xg")

    PW = 4
    SW = 2
    GW = 32
    LW = 32
    LB = 32
    TW = 32
    pe_g0, ve_g0, sc_g0, lg_g0, ld_g0, st_g0 = (cnt["pe"], cnt["ve"], cnt["sc"],
                                                cnt["lg"], cnt["ld"], cnt["st"])
    lb_g0 = cnt["lb"]

    @block.gpsimd
    def _(g):
        g.wait_ge(sems["ve"], ve_x1)
        for w in range(NKH):
            if w >= 2:
                g.wait_ge(sems["pe"], pe_g0 + (w - 1) * PW)
            dmai(g, s_qw[(w % 2) * 2][:], dr["qg"][:, w:w + 1, :].rearrange("p a b -> p (a b)"), "lg")
            dmai(g, s_qw[(w % 2) * 2 + 1][:], dr["qu"][:, w:w + 1, :].rearrange("p a b -> p (a b)"), "lg")

    dq_end = [0] * NKH
    ev_at = [[0, 0] for _ in range(NKH)]

    def emit_evict(v, w):
        v.wait_ge(sems["pe"], pe_g0 + w * PW + PW)
        v.wait_ge(sems["sc"], sc_g0 + w * SW + SW)
        if w >= 2:
            v.wait_ge(sems["st"], st_g0 + (w - 1) * TW)
        for n in range(2):
            ev_at[w][n] = inc(v.tensor_tensor(out=s_hst[(w % 2) * 2 + n][:], in0=s_gt[n][:],
                                              in1=P[4 + n][:], op=ALU.mult), "ve")

    @block.vector
    def _(v):
        for w in range(NKH):
            v.wait_ge(sems["lb"], lb_g0 + (w + 1) * LB)
            v.wait_ge(sems["lg"], lg_g0 + (w + 1) * GW)
            for wt in range(2):
                qwb = s_qw[(w % 2) * 2 + wt]
                gbb = s_gb[(w % 2) * 2 + wt]
                for b in range(8):
                    dq_end[w] = inc(v.tensor_tensor(out=qwb[:, b * 512:(b + 1) * 512],
                                                    in0=qwb[:, b * 512:(b + 1) * 512],
                                                    in1=gbb[:, b * 512:(b + 1) * 512],
                                                    op=ALU.mult), "ve")
            if w >= 1:
                emit_evict(v, w - 1)
        emit_evict(v, NKH - 1)


    def g_spill(sy, w):
        for n in range(2):
            sy.wait_ge(sems["ve"], ev_at[w][n])
            dmai(sy, hsp[n].ap()[:, w:w + 1, :].rearrange("p a b -> p (a b)"),
                 s_hst[(w % 2) * 2 + n][:], "st")

    @block.sync
    def _(sy):
        for w in range(NKH):
            if w >= 2:
                g_spill(sy, w - 2)
                sy.wait_ge(sems["pe"], pe_g0 + (w - 1) * PW)
            dmai(sy, s_nzt[w % 2][:, 0:128], nzg_d.ap()[:, w * 128:(w + 1) * 128])
            dmai(sy, s_nzt[w % 2][:, 128:256], nzu_d.ap()[:, w * 128:(w + 1) * 128])
        g_spill(sy, NKH - 2)
        g_spill(sy, NKH - 1)

    @block.scalar
    def _(s):
        def silu(w):
            for n in range(2):
                s.wait_ge(sems["pe"], pe_g0 + w * PW + n + 1)
                if w >= 1:
                    s.wait_ge(sems["ve"], ev_at[w - 1][1])
                inc(s.activation(out=s_gt[n][:], in_=P[2 + n][:], func=AF.Silu), "sc")
        for w in range(NKH):
            if w >= 2:
                s.wait_ge(sems["ve"], dq_end[w - 2])
            dmai(s, s_gb[(w % 2) * 2][:], dr["sgw"][w:w + 1, :].broadcast_to((128, V)), "lb")
            dmai(s, s_gb[(w % 2) * 2 + 1][:], dr["suw"][w:w + 1, :].broadcast_to((128, V)), "lb")
            if w >= 1:
                silu(w - 1)
        silu(NKH - 1)

    @block.tensor
    def _(t):
        t.wait_ge(sems["sc"], sc_xgv)
        t.wait_ge(sems["st"], st_x1)
        for w in range(NKH):
            t.wait_ge(sems["lg"], lg_g0 + (w + 1) * GW)
            t.wait_ge(sems["ld"], ld_g0 + (w + 1) * LW)
            t.wait_ge(sems["ve"], dq_end[w])
            if w >= 1:
                t.wait_ge(sems["sc"], sc_g0 + w * SW)
            for wt in range(2):
                if wt == 1 and w >= 1:
                    t.wait_ge(sems["ve"], ev_at[w - 1][1])
                qwb = s_qw[(w % 2) * 2 + wt]
                for n in range(2):
                    pb = P[2 + wt * 2 + n]
                    for k in range(NKV):
                        t.matmul(pb[:], qwb[:, k * 128:(k + 1) * 128],
                                 s_x1[:, k * T + n * TH:k * T + n * TH + TH],
                                 start=(k == 0), stop=False)
                    inc(t.matmul(pb[:], s_nzt[w % 2][0:32, wt * 128:wt * 128 + 128],
                                 s_xg[0:32, n * TH:n * TH + TH], start=False, stop=True), "pe")

    st_hsp = cnt["st"]
    pe_gend = cnt["pe"]
    barrier("g")

    e4.close()
    e5 = ExitStack()
    s_hh = mk(e5, "s_hh", [128, NKH * TH], BF16)
    s_qd = [mk(e5, f"s_qd_{i}", [128, NKH * 128], BF16) for i in range(2)]
    s_drow = [mk(e5, "s_drow_0", [1, HD], BF16)] * 2
    s_e86 = mk(e5, "s_e86", [128, NKH * NKH], BF16)
    s_hg = mk(e5, "s_hg", [NKH, TH], BF16)
    s_nzdt = [mk(e5, f"s_nzdt_{i}", [NKH, 128], BF16) for i in range(2)]
    s_xtt = [mk(e5, f"s_xtt_{i}", [128, TH], BF16) for i in range(2)]
    s_xfb = [mk(e5, f"s_xfb_{i}", [128, TH], BF16) for i in range(2)]
    s_sq = mk(e5, "s_sq", [128, TH], BF16)
    s_of = [mk(e5, f"s_of_{i}", [128, TH], F32) for i in range(2)]

    @block.gpsimd
    def _(g):
        g.wait_ge(sems["pe"], pe_gend)
        g.wait_ge(sems["st"], st_hsp)
        dmai(g, s_e86[:], dr["e86"], "lg")
    lg_d0p = cnt["lg"]

    for hf in range(2):
        @block.sync
        def _(sy, hf=hf):
            sy.wait_ge(sems["st"], st_hsp)
            if hf == 1:
                sy.wait_ge(sems["pe"], cnt["pe"])
            for q in range(2):
                dmai(sy, s_hh[:, q * 43 * TH:(q + 1) * 43 * TH],
                     hsp[hf].ap().rearrange("p a b -> p (a b)")[:, q * 43 * TH:(q + 1) * 43 * TH])
        ld_hh = cnt["ld"]

        @block.tensor
        def _(t, hf=hf, ld_hh=ld_hh):
            t.wait_ge(sems["ld"], ld_hh)
            if hf == 0:
                t.wait_ge(sems["lg"], lg_d0p)
            for k in range(NKH):
                ins = t.matmul(P[4][0:NKH, :], s_e86[:, k * NKH:(k + 1) * NKH],
                               s_hh[:, k * TH:(k + 1) * TH],
                               start=(k == 0), stop=(k == NKH - 1))
            inc(ins, "pe")
        pe_hg = cnt["pe"]

        @block.scalar
        def _(s, pe_hg=pe_hg):
            s.wait_ge(sems["pe"], pe_hg)
            inc(s.activation(out=s_hg[:], in_=P[4][0:NKH, :], func=AF.Copy), "sc")
        sc_hg = cnt["sc"]

        PD = DBLK + 1 + 1
        VD = DBLK + 1
        pe_d0, ve_d0, sc_d0, lg_d0, ld_d0, st_d0 = (cnt["pe"], cnt["ve"], cnt["sc"],
                                                    cnt["lg"], cnt["ld"], cnt["st"])

        @block.gpsimd
        def _(g, hf=hf, pe_d0=pe_d0):
            for m in range(NKV):
                if m >= 1:
                    g.wait_ge(sems["pe"], pe_d0 + (m - 1) * PD + DBLK)
                else:
                    g.wait_ge(sems["pe"], pe_d0)
                dmai(g, s_qd[m % 2][:], dr["qd"][:, m:m + 1, :].rearrange("p a b -> p (a b)"), "lg")
                dmai(g, s_drow[m % 2][:], dr["sdw"][m:m + 1, :], "lg")

        def d_spill(sy, m, hf, pe_d0):
            sy.wait_ge(sems["pe"], pe_d0 + m * PD + PD)
            dmai(sy, xfd[hf].ap()[:, m:m + 1, :].rearrange("p a b -> p (a b)"),
                 s_xfb[m % 2][:], "st")

        @block.sync
        def _(sy, hf=hf, pe_d0=pe_d0, ve_d0=ve_d0):
            for m in range(NKV):
                if m >= 2:
                    d_spill(sy, m - 2, hf, pe_d0)
                    sy.wait_ge(sems["pe"], pe_d0 + (m - 1) * PD)
                dmai(sy, s_nzdt[m % 2][:], nzd_d.ap()[:, m * 128:(m + 1) * 128])
                dmai(sy, s_xtt[m % 2][:],
                     x1d.ap()[:, m:m + 1, hf * TH:hf * TH + TH].rearrange("p a b -> p (a b)"))
            d_spill(sy, NKV - 2, hf, pe_d0)
            d_spill(sy, NKV - 1, hf, pe_d0)

        def dq_posd(mm, b, ve_d0=ve_d0):
            return ve_d0 + mm * VD + b + 1

        @block.tensor
        def _(t, hf=hf, sc_hg=sc_hg, pe_d0=pe_d0, ve_d0=ve_d0, lg_d0=lg_d0, ld_d0=ld_d0):
            t.wait_ge(sems["sc"], sc_hg)
            for m in range(NKV):
                t.wait_ge(sems["lg"], lg_d0 + (m + 1) * 32)
                t.wait_ge(sems["ld"], ld_d0 + (m + 1) * 32)
                for b in range(DBLK):
                    blk = m * DBLK + b
                    if blk >= 2:
                        t.wait_ge(sems["ve"], dq_posd((blk - 2) // DBLK, (blk - 2) % DBLK))
                    nn = min(512, HD - b * 512)
                    inc(t.matmul(P[b % 2][0:128, 0:nn], s_onrb[:],
                                 s_drow[m % 2][0:1, b * 512:b * 512 + nn],
                                 start=True, stop=True), "pe")
                t.wait_ge(sems["ve"], ve_d0 + m * VD + DBLK)
                if m >= 2:
                    t.wait_ge(sems["ve"], ve_d0 + (m - 2) * VD + VD)
                pb = P[2 + m % 2]
                for k in range(NKH):
                    t.matmul(pb[:], s_qd[m % 2][:, k * 128:(k + 1) * 128],
                             s_hh[:, k * TH:(k + 1) * TH],
                             start=(k == 0), stop=False)
                inc(t.matmul(pb[:], s_nzdt[m % 2][:], s_hg[:],
                             start=False, stop=True), "pe")
                t.wait_ge(sems["ve"], ve_d0 + m * VD + VD)
                t.wait_ge(sems["sc"], sc_d0 + m + 1)
                t.matmul(P[5][0:1, :], s_onc[:], s_xfb[m % 2][:],
                         start=(m == 0), stop=(m == NKV - 1))
                inc(t.matmul(P[5][32:33, :], s_onc[:], s_sq[:],
                             start=(m == 0), stop=(m == NKV - 1)), "pe")

        @block.vector
        def _(v, hf=hf, pe_d0=pe_d0, sc_d0=sc_d0, ld_d0=ld_d0, st_d0=st_d0):
            for m in range(NKV):
                for b in range(DBLK):
                    v.wait_ge(sems["pe"], pe_d0 + m * PD + b + 1)
                    nn = min(512, HD - b * 512)
                    inc(v.tensor_tensor(out=s_qd[m % 2][:, b * 512:b * 512 + nn],
                                        in0=s_qd[m % 2][:, b * 512:b * 512 + nn],
                                        in1=P[b % 2][0:128, 0:nn], op=ALU.mult), "ve")
                v.wait_ge(sems["pe"], pe_d0 + m * PD + DBLK + 1)
                v.wait_ge(sems["ld"], ld_d0 + (m + 1) * 32)
                if m >= 2:
                    v.wait_ge(sems["st"], st_d0 + (m - 1) * 16)
                inc(v.tensor_tensor(out=s_xfb[m % 2][:], in0=P[2 + m % 2][:],
                                    in1=s_xtt[m % 2][:], op=ALU.add), "ve")

        @block.scalar
        def _(s, hf=hf, ve_d0=ve_d0):
            for m in range(NKV):
                s.wait_ge(sems["ve"], ve_d0 + m * VD + VD)
                inc(s.activation(out=s_sq[:], in_=s_xfb[m % 2][:], func=AF.Square), "sc")

        pe_dend = cnt["pe"]

        @block.vector
        def _(v, pe_dend=pe_dend):
            v.wait_ge(sems["pe"], pe_dend)
            v.tensor_scalar(out=s_row[0:1, 0:TH], in0=P[5][0:1, :], scalar1=1.0 / V,
                            scalar2=None, op0=ALU.mult)
            v.tensor_scalar(out=s_row[0:1, 1024:1024 + TH], in0=P[5][32:33, :],
                            scalar1=1.0 / V, scalar2=None, op0=ALU.mult)
            v.drain()
            v.tensor_tensor(out=s_row[0:1, 2048:2048 + TH], in0=s_row[0:1, 0:TH],
                            in1=s_row[0:1, 0:TH], op=ALU.mult)
            v.drain()
            inc(v.tensor_tensor(out=s_row[0:1, 1024:1024 + TH],
                                in0=s_row[0:1, 1024:1024 + TH],
                                in1=s_row[0:1, 2048:2048 + TH], op=ALU.subtract), "ve")
        ve_r5 = cnt["ve"]

        @block.scalar
        def _(s, ve_r5=ve_r5):
            s.wait_ge(sems["ve"], ve_r5)
            inc(s.activation(out=s_row[0:1, 1024:1024 + TH], in_=s_row[0:1, 1024:1024 + TH],
                             func=AF.Sqrt, bias=s_cst[0:1, 0:1], scale=1.0), "sc")
        sc_r5 = cnt["sc"]

        @block.vector
        def _(v, sc_r5=sc_r5):
            v.wait_ge(sems["sc"], sc_r5)
            inc(v.reciprocal(out=s_row[0:1, 1024:1024 + TH],
                             in_=s_row[0:1, 1024:1024 + TH]), "ve")
        ve_r5b = cnt["ve"]

        @block.tensor
        def _(t, ve_r5b=ve_r5b):
            t.wait_ge(sems["ve"], ve_r5b)
            t.matmul(P[6][:], s_onrf[:], s_row[0:1, 0:TH], start=True, stop=True)
            inc(t.matmul(P[7][:], s_onrf[:], s_row[0:1, 1024:1024 + TH],
                         start=True, stop=True), "pe")
        pe_bc5 = cnt["pe"]
        ld_f0 = cnt["ld"]
        ve_f0 = cnt["ve"]
        st_f0 = cnt["st"]

        def f_store(sy, m, hf):
            sy.wait_ge(sems["ve"], ve_f0 + m * 2 + 2)
            dmai(sy, outT[m * 128:(m + 1) * 128, hf * TH:hf * TH + TH],
                 s_of[m % 2][:], "st")

        @block.sync
        def _(sy, hf=hf, ve_f0=ve_f0, st_d0=st_d0):
            sy.wait_ge(sems["st"], st_d0 + NKV * 16)
            for m in range(NKV):
                if m >= 2:
                    f_store(sy, m - 2, hf)
                    sy.wait_ge(sems["ve"], ve_f0 + (m - 2) * 2 + 2)
                dmai(sy, s_xfb[m % 2][:],
                     xfd[hf].ap()[:, m:m + 1, :].rearrange("p a b -> p (a b)"))
            f_store(sy, NKV - 2, hf)
            f_store(sy, NKV - 1, hf)

        @block.vector
        def _(v, hf=hf, pe_bc5=pe_bc5, ld_f0=ld_f0, st_f0=st_f0):
            v.wait_ge(sems["pe"], pe_bc5)
            for m in range(NKV):
                v.wait_ge(sems["ld"], ld_f0 + (m + 1) * 16)
                if m >= 2:
                    v.wait_ge(sems["st"], st_f0 + (m - 1) * 16)
                v.tensor_tensor(out=s_of[m % 2][:], in0=s_xfb[m % 2][:], in1=P[6][:],
                                op=ALU.subtract)
                v.drain()
                inc(v.tensor_tensor(out=s_of[m % 2][:], in0=s_of[m % 2][:], in1=P[7][:],
                                    op=ALU.mult), "ve", 2)

        barrier(f"hf{hf}")

    @block.sync
    def _(sy):
        sy.wait_ge(sems["st"], cnt["st"])

    e5.close()
    es.close()
    return nc


def _stage(inputs, core):
    b, ch = core // 4, core % 4
    ip = {}
    x = inputs["L0_attn_x"][b]
    xT = np.ascontiguousarray(x.T[:, ch * T:(ch + 1) * T])
    ip["xT"] = xT
    sel = np.zeros((128, 4), np.float32)
    sel[:, ch] = 1.0
    ip["selv"] = sel
    Win = inputs["L0_attn_Win"]
    ip["winT"] = np.ascontiguousarray(
        Win.T.reshape(NKV, 128, DIM).transpose(1, 0, 2).reshape(128, NKV * DIM))
    W1 = inputs["L0_attn_W1"]
    ip["w1T"] = np.ascontiguousarray(
        W1.transpose(0, 2, 1).reshape(DEPTH, NKD, 128, DFF).transpose(0, 2, 1, 3)
        .reshape(DEPTH, 128, NKD * DFF))
    W2 = inputs["L0_attn_W2"]
    ip["w2T"] = np.ascontiguousarray(
        W2.transpose(0, 2, 1).reshape(DEPTH, NKF, 128, DIM).transpose(0, 2, 1, 3)
        .reshape(DEPTH, 128, NKF * DIM))
    Wout = inputs["L0_attn_Wout"]
    ip["woutT"] = np.ascontiguousarray(
        Wout.T.reshape(NKD, 128, V).transpose(1, 0, 2).reshape(128, NKD * V))
    pars = []
    for nm, key in [("ln1w", "L0_attn_ln1_w"), ("ln1b", "L0_attn_ln1_b"),
                    ("ln2w", "L0_attn_ln2_w"), ("ln2b", "L0_attn_ln2_b"),
                    ("pre", "L0_attn_phz_re"), ("pim", "L0_attn_phz_im"),
                    ("lre", "L0_attn_lc_re"), ("lim", "L0_attn_lc_im"),
                    ("scl", "L0_attn_sc_lin")]:
        a = inputs[key]
        pars.append(a.T.reshape(NKD, 128, DEPTH).transpose(1, 0, 2))
    ip["par"] = np.ascontiguousarray(np.stack(pars, axis=1).reshape(128, 9 * NKD * DEPTH))
    lnl = np.stack([inputs["L0_attn_lnl_w"], inputs["L0_attn_lnl_b"]], 0)
    ip["lnl"] = np.ascontiguousarray(
        lnl.reshape(2, NKD, 128).transpose(2, 0, 1).reshape(128, 2 * NKD))
    ip["bin"] = np.ascontiguousarray(inputs["L0_attn_bin"].reshape(NKD, 128).T)
    ip["bout"] = np.ascontiguousarray(inputs["L0_attn_bout"].reshape(NKV, 128).T)
    ip["b1"] = np.ascontiguousarray(
        inputs["L0_attn_b1"].T.reshape(NKF, 128, DEPTH).transpose(1, 0, 2)
        .reshape(128, NKF * DEPTH))
    ip["b2"] = np.ascontiguousarray(
        inputs["L0_attn_b2"].T.reshape(NKD, 128, DEPTH).transpose(1, 0, 2)
        .reshape(128, NKD * DEPTH))
    qg = inputs["L0_ffn_qw_gate"]
    ip["qg"] = np.ascontiguousarray(
        qg.reshape(NKV, 128, NKH, 128).transpose(1, 2, 0, 3)
        .reshape(128, NKH, NKV * 128).astype(np.uint8))
    qu = inputs["L0_ffn_qw_up"]
    ip["qu"] = np.ascontiguousarray(
        qu.reshape(NKV, 128, NKH, 128).transpose(1, 2, 0, 3)
        .reshape(128, NKH, NKV * 128).astype(np.uint8))
    qd = inputs["L0_ffn_qw_down"]
    ip["qd"] = np.ascontiguousarray(
        qd.reshape(NKH, 128, NKV, 128).transpose(1, 2, 0, 3)
        .reshape(128, NKV, NKH * 128).astype(np.uint8))
    sgn = inputs["L0_ffn_s_gate"]
    ip["sgw"] = np.ascontiguousarray(
        sgn.reshape(32, NKH, 128).transpose(1, 0, 2).reshape(NKH, V))
    sun = inputs["L0_ffn_s_up"]
    ip["suw"] = np.ascontiguousarray(
        sun.reshape(32, NKH, 128).transpose(1, 0, 2).reshape(NKH, V))
    sdn = inputs["L0_ffn_s_down"]
    ip["sdw"] = np.ascontiguousarray(
        sdn.reshape(NKH, NKV, 128).transpose(1, 0, 2).reshape(NKV, HD))
    ip["zg"] = inputs["L0_ffn_z_gate"].astype(np.uint8)
    ip["zu"] = inputs["L0_ffn_z_up"].astype(np.uint8)
    ip["zd"] = inputs["L0_ffn_z_down"].astype(np.uint8)
    ip["sg"] = sgn
    ip["su"] = sun
    ip["sd"] = sdn
    e32 = np.zeros((32, 128, 32), np.float32)
    for k in range(32):
        e32[k, :, k] = 1.0
    ip["e32"] = np.ascontiguousarray(e32.transpose(1, 0, 2).reshape(128, 32 * 32))
    e86 = np.zeros((NKH, 128, NKH), np.float32)
    for k in range(NKH):
        e86[k, :, k] = 1.0
    ip["e86"] = np.ascontiguousarray(e86.transpose(1, 0, 2).reshape(128, NKH * NKH))
    cst = np.zeros((128, 2), np.float32)
    cst[:, 0] = EPS
    cst[:, 1] = 1e-30
    ip["cst"] = cst
    ip["onc"] = np.ones((128, 1), np.float32)
    ip["onr"] = np.ones((1, 128), np.float32)
    return ip


_NC_CACHE = {}


def kernel(**inputs):
    dbg = bool(inputs.pop("_dbg", False))
    key = dbg
    if key not in _NC_CACHE:
        _NC_CACHE[key] = build_nc(dbg=dbg)
    nc = _NC_CACHE[key]
    in_maps = [_stage(inputs, c) for c in range(8)]
    res = run_bass_kernel_spmd(nc, in_maps, core_ids=list(range(8)))
    out = np.empty((B, L, V), np.float32)
    for c in range(8):
        b, ch = c // 4, c % 4
        out[b, ch * T:(ch + 1) * T, :] = res.results[c]["outT"].T
    if dbg:
        return out, res
    return out
